# version 08
# baseline (speedup 1.0000x reference)
import os
os.environ.setdefault("NEURON_SCRATCHPAD_PAGE_SIZE", "512")

import numpy as np
import ml_dtypes
from contextlib import ExitStack

import concourse.bass as bass
import concourse.mybir as mybir
from concourse.bass_utils import run_bass_kernel_spmd

F32 = mybir.dt.float32
BF16 = mybir.dt.bfloat16
U16 = mybir.dt.uint16
AF = mybir.ActivationFunctionType
ALU = mybir.AluOpType
AX = mybir.AxisListType
BF = ml_dtypes.bfloat16

NCORE = 8
TOKC = 2048
HD = 4096
MT = 14336
MSL = MT // NCORE
L = 4096
NST = 64
HCN = 32
LMCN = 14
GMCN = 112
TBN = 4
LNBN = 16
LNT = 128

NF4 = [-1.0, -0.6961928009986877, -0.5250730514526367, -0.39491748809814453,
       -0.28444138169288635, -0.18477343022823334, -0.09105003625154495, 0.0,
       0.07958029955625534, 0.16093020141124725, 0.24611230194568634,
       0.33791524171829224, 0.44070982933044434, 0.5626170039176941,
       0.7229568362236023, 1.0]

TILE_E = 128 * 128
GU_CH = LMCN * 8 * TILE_E
D_CH = LMCN * 16 * TILE_E


class Ctr:
    def __init__(self, stack, nc, name):
        self.h = stack.enter_context(nc.semaphore(name))
        self.n = 0


def build_program():
    nc = bass.Bass(target_bir_lowering=False, detect_race_conditions=False)

    xT = nc.dram_tensor("xT", [HD, TOKC], F32, kind="ExternalInput")
    gq = nc.dram_tensor("gq", [HD, MSL], U16, kind="ExternalInput")
    uq = nc.dram_tensor("uq", [HD, MSL], U16, kind="ExternalInput")
    dq = nc.dram_tensor("dq", [MSL, HD], U16, kind="ExternalInput")
    amg = nc.dram_tensor("amg", [64, MSL], F32, kind="ExternalInput")
    amu = nc.dram_tensor("amu", [64, MSL], F32, kind="ExternalInput")
    amd = nc.dram_tensor("amd", [28, HD], F32, kind="ExternalInput")
    rre = nc.dram_tensor("rre", [128, 4], F32, kind="ExternalInput")
    rim = nc.dram_tensor("rim", [128, 4], F32, kind="ExternalInput")
    cre = nc.dram_tensor("cre", [128, 4], F32, kind="ExternalInput")
    cim = nc.dram_tensor("cim", [128, 4], F32, kind="ExternalInput")
    lamre = nc.dram_tensor("lamre", [NST], F32, kind="ExternalInput")
    lamim = nc.dram_tensor("lamim", [NST], F32, kind="ExternalInput")
    ctre = nc.dram_tensor("ctre", [NST], F32, kind="ExternalInput")
    ctim = nc.dram_tensor("ctim", [NST], F32, kind="ExternalInput")
    pre_ = nc.dram_tensor("pre_", [NST], F32, kind="ExternalInput")
    pim = nc.dram_tensor("pim", [NST], F32, kind="ExternalInput")
    qre = nc.dram_tensor("qre", [NST], F32, kind="ExternalInput")
    qim = nc.dram_tensor("qim", [NST], F32, kind="ExternalInput")
    bp = nc.dram_tensor("bp", [NST], F32, kind="ExternalInput")
    vcre = nc.dram_tensor("vcre", [NST, NST], F32, kind="ExternalInput")
    vcim = nc.dram_tensor("vcim", [NST, NST], F32, kind="ExternalInput")
    dsc = nc.dram_tensor("dsc", [1], F32, kind="ExternalInput")
    lsc = nc.dram_tensor("lsc", [1], F32, kind="ExternalInput")
    cosm = nc.dram_tensor("cosm", [L, 512], BF16, kind="ExternalInput")
    nsinm = nc.dram_tensor("nsinm", [L, 512], BF16, kind="ExternalInput")
    jrev = nc.dram_tensor("jrev", [128, 128], BF16, kind="ExternalInput")

    yT = nc.dram_tensor("yT", [HD, TOKC], F32, kind="ExternalOutput")

    agar_in = nc.dram_tensor("agar_in", [1, 1024], BF16, kind="Internal")
    agar_out = nc.dram_tensor("agar_out", [8, 1024], BF16, kind="Internal")
    agk_in = nc.dram_tensor("agk_in", [1, 512], F32, kind="Internal")
    agk_out = nc.dram_tensor("agk_out", [8, 512], F32, kind="Internal")
    aggu_in = nc.dram_tensor("aggu_in", [8 * GU_CH // 1024, 1024], BF16, kind="Internal")
    aggu_out = nc.dram_tensor("aggu_out", [64 * GU_CH // 1024, 1024], BF16, kind="Internal")
    agd_in = nc.dram_tensor("agd_in", [2 * D_CH // 1024, 1024], BF16, kind="Internal")
    agd_out = nc.dram_tensor("agd_out", [16 * D_CH // 1024, 1024], BF16, kind="Internal")
    h_s = nc.dram_tensor("h_s", [MT, TOKC], BF16, kind="Internal")
    conv_s = nc.dram_tensor("conv_s", [HD, TOKC], F32, kind="Internal")
    kpad = nc.dram_tensor("kpad", [1, 4352], BF16, kind="Internal")
    bounce = nc.dram_tensor("bounce", [1, 8192], F32, kind="Internal")

    with ExitStack() as st:
        sv = Ctr(st, nc, "sv")
        ss = Ctr(st, nc, "ss")
        sp = Ctr(st, nc, "sp")
        c_cst = Ctr(st, nc, "c_cst")
        c_qt2 = [Ctr(st, nc, "c_qt%d" % i) for i in range(2)]
        c_cos4 = [Ctr(st, nc, "c_cos%d" % i) for i in range(4)]
        c_xb = Ctr(st, nc, "c_xb")
        c_am = Ctr(st, nc, "c_am")
        c_agw2 = [Ctr(st, nc, "c_agw%d" % i) for i in range(2)]
        c_mr = Ctr(st, nc, "c_mr")
        c_vr = Ctr(st, nc, "c_vr")
        c_s2 = Ctr(st, nc, "c_s2")
        c_misc = Ctr(st, nc, "c_misc")
        c_ar = Ctr(st, nc, "c_ar")
        c_agi = Ctr(st, nc, "c_agi")
        c_agk = Ctr(st, nc, "c_agk")
        c_kpd = Ctr(st, nc, "c_kpd")
        c_fwd2 = [Ctr(st, nc, "c_fwd%d" % i) for i in range(2)]
        c_pan2 = [Ctr(st, nc, "c_pan%d" % i) for i in range(2)]
        c_hsp2 = [Ctr(st, nc, "c_hsp%d" % i) for i in range(2)]
        c_csp2 = [Ctr(st, nc, "c_csp%d" % i) for i in range(2)]
        c_hp = Ctr(st, nc, "c_hp")
        c_wdp2 = [Ctr(st, nc, "c_wdp%d" % i) for i in range(2)]
        c_ld2 = [Ctr(st, nc, "c_ld%d" % i) for i in range(2)]
        so2 = [Ctr(st, nc, "so2_%d" % i) for i in range(2)]
        k_ar = st.enter_context(nc.semaphore("k_ar"))
        k_k = st.enter_context(nc.semaphore("k_k"))
        k_gu = [st.enter_context(nc.semaphore("k_gu%d" % i)) for i in range(8)]
        k_d = [st.enter_context(nc.semaphore("k_d%d" % i)) for i in range(2)]

        blocks = []

        def new_block():
            b = {k: [] for k in ("sync", "scalar", "vector", "tensor", "gpsimd")}
            blocks.append(b)
            return b

        def W(b, eng, ctr, val):
            b[eng].append(lambda e, s=ctr.h, v=val: e.wait_ge(s, v))

        def WS(b, eng, sem, val=1):
            b[eng].append(lambda e, s=sem, v=val: e.wait_ge(s, v))

        def DMA(b, eng, ctr, out, in_, **kw):
            ctr.n += 16
            b[eng].append(lambda e, o=out, i=in_, s=ctr.h, k=dict(kw):
                          e.dma_start(out=o, in_=i, **k).then_inc(s, 16))
            return ctr.n

        def DMAT(b, eng, ctr, out, in_):
            ctr.n += 16
            b[eng].append(lambda e, o=out, i=in_, s=ctr.h:
                          e.dma_start_transpose(out=o, in_=i).then_inc(s, 16))
            return ctr.n

        def V(b, f):
            sv.n += 1
            b["vector"].append(lambda e, f=f, s=sv.h: f(e).then_inc(s, 1))
            return sv.n

        def VS(b, f):
            n = V(b, f)
            W(b, "vector", sv, n)
            return n

        def S(b, f):
            ss.n += 1
            b["scalar"].append(lambda e, f=f, s=ss.h: f(e).then_inc(s, 1))
            return ss.n

        def SS(b, f):
            n = S(b, f)
            W(b, "scalar", ss, n)
            return n

        def TN(b, f):
            b["tensor"].append(lambda e, f=f: f(e))

        def TI(b, f):
            sp.n += 1
            b["tensor"].append(lambda e, f=f, s=sp.h: f(e).then_inc(s, 1))
            return sp.n

        def CC(b, sem, in_ap, out_ap):
            b["gpsimd"].append(lambda e, s=sem, i=in_ap, o=out_ap:
                               e.collective_compute(
                                   "AllGather", ALU.bypass,
                                   replica_groups=[list(range(NCORE))],
                                   ins=[i.opt()], outs=[o.opt()]).then_inc(s))

        ones = st.enter_context(nc.sbuf_tensor("ones", [128, 1], BF16))
        coefc = st.enter_context(nc.sbuf_tensor("coefc", [128, 16], F32))
        sbias = st.enter_context(nc.sbuf_tensor("sbias", [128, 8], F32))
        jrev_sb = st.enter_context(nc.sbuf_tensor("jrev_sb", [128, 128], BF16))
        dcol = st.enter_context(nc.sbuf_tensor("dcol", [128, 1], F32))
        zrow = st.enter_context(nc.sbuf_tensor("zrow", [1, 128], BF16))

        ac_xn = ExitStack()
        xn = ac_xn.enter_context(nc.sbuf_tensor("xn", [128, HCN * TOKC], BF16))

        ac_dq = ExitStack()
        sbq = lambda name, shape, dt: ac_dq.enter_context(nc.sbuf_tensor(name, shape, dt))
        qt = [sbq("qt%d" % i, [128, 2048], U16) for i in range(2)]
        qb = sbq("qb", [128, 2048], BF16)
        am = sbq("am", [128, 2048], F32)
        dacc = sbq("dacc", [128, 1024], F32)
        mk = [sbq("mk%d" % i, [128, 1024], BF16) for i in range(3)]
        dout = [sbq("dout%d" % i, [128, 2048], BF16) for i in range(2)]

        b0 = new_block()
        DMA(b0, "sync", c_cst, jrev_sb[:, :], jrev.ap())
        DMA(b0, "sync", c_cst, dcol[:, :], bass.AP(dsc, 0, [[0, 128], [1, 1]]))
        V(b0, lambda e: e.memset(ones[:, :], 1.0))
        for k in range(1, 16):
            dk = NF4[k] - NF4[k - 1]
            cf = dk / 2.0 if (k % 2 == 1) else dk
            V(b0, lambda e, k=k, cf=cf: e.memset(coefc[:, k:k + 1], cf))
        for j in range(8):
            V(b0, lambda e, j=j: e.memset(sbias[:, j:j + 1], -(2 * j + 1 - 0.5)))
        n_setup_v = V(b0, lambda e: e.memset(zrow[:, :], 0.0))
        W(b0, "gpsimd", sv, n_setup_v)
        DMA(b0, "gpsimd", c_misc, kpad[0:1, 0:128], zrow[0:1, :])
        DMA(b0, "gpsimd", c_misc, kpad[0:1, 4224:4352], zrow[0:1, :])
        W(b0, "sync", c_cst, c_cst.n)
        W(b0, "gpsimd", c_misc, c_misc.n)

        ac1 = ExitStack()
        sb = lambda name, shape, dt: ac1.enter_context(nc.sbuf_tensor(name, shape, dt))
        rre_sb = sb("rre_sb", [128, 4], F32)
        rim_sb = sb("rim_sb", [128, 4], F32)
        cre_sb = sb("cre_sb", [128, 4], F32)
        cim_sb = sb("cim_sb", [128, 4], F32)
        lamre_b = sb("lamre_b", [128, NST], F32)
        lamim_b = sb("lamim_b", [128, NST], F32)
        vc_re = sb("vc_re", [NST, NST], F32)
        vc_im = sb("vc_im", [NST, NST], F32)
        bpb = sb("bpb", [NST, NST], F32)
        colc = sb("colc", [NST, 8], F32)
        vtmp = sb("vtmp", [NST, 4], F32)
        vcols = sb("vcols", [NST, 8], F32)
        vrep = sb("vrep", [128, 8 * NST], F32)
        s2col = sb("s2col", [128, 1], F32)
        gcr = sb("gcr", [128, 1], F32)
        gci = sb("gci", [128, 1], F32)
        dre = sb("dre", [128, NST], F32)
        dim_ = sb("dim_", [128, NST], F32)
        d2 = sb("d2", [128, NST], F32)
        dt2 = sb("dt2", [128, NST], F32)
        inv = sb("inv", [128, NST], F32)
        dri = sb("dri", [128, NST], F32)
        dii = sb("dii", [128, NST], F32)
        kt1 = sb("kt1", [128, NST], F32)
        kt2 = sb("kt2", [128, NST], F32)
        kt3 = sb("kt3", [128, NST], F32)
        kacc = sb("kacc", [128, 8 * 4], F32)
        u1 = sb("u1", [128, 4], F32)
        u2 = sb("u2", [128, 4], F32)
        u3 = sb("u3", [128, 4], F32)
        u4 = sb("u4", [128, 4], F32)
        u5 = sb("u5", [128, 4], F32)
        u6 = sb("u6", [128, 4], F32)
        arre = sb("arre", [128, 4], F32)
        arim = sb("arim", [128, 4], F32)
        arbre = sb("arbre", [128, 32], BF16)
        arbim = sb("arbim", [128, 32], BF16)
        ls_sb = sb("ls_sb", [1, 1], F32)
        e_sb = sb("e_sb", [1, 1], F32)
        t2_sb = sb("t2_sb", [1, 1], F32)
        costl = [sb("costl%d" % i, [128, 512], BF16) for i in range(4)]
        krow = sb("krow", [1, 512], F32)
        bands = sb("bands", [128, 32 * 128], BF16)
        fwd = [sb("fwd%d" % i, [128, 128], BF16) for i in range(2)]
        cvst = [sb("cvst%d" % i, [128, 512], F32) for i in range(2)]
        xb = sb("xb", [128, 4096], BF16)
        xb2 = sb("xb2", [128, 1024], BF16)
        mursrow = sb("mursrow", [1, 256], F32)
        vt1 = sb("vt1", [1, 128], F32)
        vt2 = sb("vt2", [1, 128], F32)
        lnrow = sb("lnrow", [1, 128], F32)
        mursrep = sb("mursrep", [128, 256], F32)

        ps1 = ExitStack()
        ps_s1 = [ps1.enter_context(nc.psum_tensor("ps_s1%d" % i, [1, 128], F32)) for i in range(2)]
        ps_s2 = [ps1.enter_context(nc.psum_tensor("ps_s2%d" % i, [1, 128], F32)) for i in range(2)]
        ps_k = ps1.enter_context(nc.psum_tensor("ps_k", [1, 512], F32))
        pc = [ps1.enter_context(nc.psum_tensor("pc%d" % i, [128, 512], F32)) for i in range(2)]

        bm = new_block()

        for dst, src in ((rre_sb, rre), (rim_sb, rim), (cre_sb, cre), (cim_sb, cim),
                         (vc_re, vcre), (vc_im, vcim)):
            DMA(bm, "sync", c_cst, dst[:, :], src.ap())
        DMA(bm, "sync", c_cst, lamre_b[:, :], bass.AP(lamre, 0, [[0, 128], [1, NST]]))
        DMA(bm, "sync", c_cst, lamim_b[:, :], bass.AP(lamim, 0, [[0, 128], [1, NST]]))
        DMA(bm, "sync", c_cst, bpb[:, :], bass.AP(bp, 0, [[0, NST], [1, NST]]))
        for i, t in enumerate((ctre, ctim, pre_, pim, qre, qim)):
            DMA(bm, "sync", c_cst, colc[:, i:i + 1], bass.AP(t, 0, [[1, NST], [1, 1]]))
        DMA(bm, "sync", c_cst, ls_sb[:, :], bass.AP(lsc, 0, [[1, 1], [1, 1]]))
        N_ACONST = c_cst.n

        W(bm, "scalar", c_cst, N_ACONST)
        SS(bm, lambda e: e.activation(e_sb[:, :], ls_sb[:, :], AF.Exp, scale=-1.0))
        n_t2 = SS(bm, lambda e: e.mul(t2_sb[:, :], e_sb[:, :], 2.0))
        W(bm, "gpsimd", ss, n_t2)
        n_bs2 = DMA(bm, "gpsimd", c_misc, bounce[0:1, 512:513], t2_sb[:, :])
        W(bm, "gpsimd", c_misc, n_bs2)
        n_s2c = DMA(bm, "gpsimd", c_s2, s2col[:, :], bass.AP(bounce, 512, [[0, 128], [1, 1]]))

        W(bm, "vector", c_cst, N_ACONST)
        VS(bm, lambda e: e.tensor_tensor(out=dre[0:NST, :], in0=vc_re[:, :], in1=bpb[:, :], op=ALU.mult))
        VS(bm, lambda e: e.reduce_sum(out=colc[:, 6:7], in_=dre[0:NST, :], axis=AX.X))
        VS(bm, lambda e: e.tensor_tensor(out=dre[0:NST, :], in0=vc_im[:, :], in1=bpb[:, :], op=ALU.mult))
        VS(bm, lambda e: e.reduce_sum(out=colc[:, 7:8], in_=dre[0:NST, :], axis=AX.X))
        vdefs = [(0, 1, 6, 7), (0, 1, 2, 3), (4, 5, 6, 7), (4, 5, 2, 3)]
        nv = None
        for vi, (ia, ib, ic, id_) in enumerate(vdefs):
            VS(bm, lambda e, ia=ia, ic=ic: e.tensor_tensor(out=vtmp[:, 0:1], in0=colc[:, ia:ia + 1], in1=colc[:, ic:ic + 1], op=ALU.mult))
            VS(bm, lambda e, ib=ib, id_=id_: e.tensor_tensor(out=vtmp[:, 1:2], in0=colc[:, ib:ib + 1], in1=colc[:, id_:id_ + 1], op=ALU.mult))
            VS(bm, lambda e, vi=vi: e.tensor_tensor(out=vcols[:, 2 * vi:2 * vi + 1], in0=vtmp[:, 0:1], in1=vtmp[:, 1:2], op=ALU.add))
            VS(bm, lambda e, ia=ia, id_=id_: e.tensor_tensor(out=vtmp[:, 2:3], in0=colc[:, ia:ia + 1], in1=colc[:, id_:id_ + 1], op=ALU.mult))
            VS(bm, lambda e, ib=ib, ic=ic: e.tensor_tensor(out=vtmp[:, 3:4], in0=colc[:, ib:ib + 1], in1=colc[:, ic:ic + 1], op=ALU.mult))
            nv = VS(bm, lambda e, vi=vi: e.tensor_tensor(out=vcols[:, 2 * vi + 1:2 * vi + 2], in0=vtmp[:, 2:3], in1=vtmp[:, 3:4], op=ALU.subtract))
        W(bm, "gpsimd", sv, nv)
        nbo = 0
        for k in range(8):
            nbo = DMA(bm, "gpsimd", c_misc, bass.AP(bounce, k * 64, [[1, NST]]), vcols[:, k:k + 1])
        W(bm, "gpsimd", c_misc, nbo)
        nvr = 0
        for k in range(8):
            nvr = DMA(bm, "gpsimd", c_vr, vrep[:, k * NST:(k + 1) * NST],
                      bass.AP(bounce, k * 64, [[0, 128], [1, NST]]))
        W(bm, "vector", c_vr, nvr)
        W(bm, "vector", c_s2, n_s2c)
        for c in range(4):
            VS(bm, lambda e, c=c: e.tensor_tensor(out=gcr[:, :], in0=rre_sb[:, c:c + 1], in1=s2col[:, :], op=ALU.mult))
            VS(bm, lambda e, c=c: e.tensor_tensor(out=gci[:, :], in0=rim_sb[:, c:c + 1], in1=s2col[:, :], op=ALU.mult))
            VS(bm, lambda e: e.tensor_scalar(out=dre[:, :], in0=lamre_b[:, :], scalar1=-1.0,
                                             scalar2=gcr[:, 0:1], op0=ALU.mult, op1=ALU.add))
            VS(bm, lambda e: e.tensor_scalar(out=dim_[:, :], in0=lamim_b[:, :], scalar1=-1.0,
                                             scalar2=gci[:, 0:1], op0=ALU.mult, op1=ALU.add))
            VS(bm, lambda e: e.tensor_tensor(out=d2[:, :], in0=dre[:, :], in1=dre[:, :], op=ALU.mult))
            VS(bm, lambda e: e.tensor_tensor(out=dt2[:, :], in0=dim_[:, :], in1=dim_[:, :], op=ALU.mult))
            VS(bm, lambda e: e.tensor_tensor(out=d2[:, :], in0=d2[:, :], in1=dt2[:, :], op=ALU.add))
            VS(bm, lambda e: e.reciprocal(out=inv[:, :], in_=d2[:, :]))
            VS(bm, lambda e: e.tensor_tensor(out=dri[:, :], in0=dre[:, :], in1=inv[:, :], op=ALU.mult))
            VS(bm, lambda e: e.tensor_tensor(out=dii[:, :], in0=dim_[:, :], in1=inv[:, :], op=ALU.mult))
            for k4 in range(4):
                ar_ = vrep[:, (2 * k4) * NST:(2 * k4 + 1) * NST]
                ai_ = vrep[:, (2 * k4 + 1) * NST:(2 * k4 + 2) * NST]
                VS(bm, lambda e, a=ar_: e.tensor_tensor(out=kt1[:, :], in0=a, in1=dri[:, :], op=ALU.mult))
                VS(bm, lambda e, a=ai_: e.tensor_tensor(out=kt2[:, :], in0=a, in1=dii[:, :], op=ALU.mult))
                VS(bm, lambda e: e.tensor_tensor(out=kt3[:, :], in0=kt1[:, :], in1=kt2[:, :], op=ALU.add))
                VS(bm, lambda e, k4=k4, c=c: e.reduce_sum(out=kacc[:, (2 * k4) * 4 + c:(2 * k4) * 4 + c + 1], in_=kt3[:, :], axis=AX.X))
                VS(bm, lambda e, a=ai_: e.tensor_tensor(out=kt1[:, :], in0=a, in1=dri[:, :], op=ALU.mult))
                VS(bm, lambda e, a=ar_: e.tensor_tensor(out=kt2[:, :], in0=a, in1=dii[:, :], op=ALU.mult))
                VS(bm, lambda e: e.tensor_tensor(out=kt3[:, :], in0=kt1[:, :], in1=kt2[:, :], op=ALU.subtract))
                VS(bm, lambda e, k4=k4, c=c: e.reduce_sum(out=kacc[:, (2 * k4 + 1) * 4 + c:(2 * k4 + 1) * 4 + c + 1], in_=kt3[:, :], axis=AX.X))
        k00re, k00im = kacc[:, 0:4], kacc[:, 4:8]
        k01re, k01im = kacc[:, 8:12], kacc[:, 12:16]
        k10re, k10im = kacc[:, 16:20], kacc[:, 20:24]
        k11re, k11im = kacc[:, 24:28], kacc[:, 28:32]
        VS(bm, lambda e: e.tensor_scalar_add(out=u1[:, :], in0=k11re, scalar1=1.0))
        VS(bm, lambda e: e.tensor_tensor(out=u2[:, :], in0=u1[:, :], in1=u1[:, :], op=ALU.mult))
        VS(bm, lambda e: e.tensor_tensor(out=u3[:, :], in0=k11im, in1=k11im, op=ALU.mult))
        VS(bm, lambda e: e.tensor_tensor(out=u2[:, :], in0=u2[:, :], in1=u3[:, :], op=ALU.add))
        VS(bm, lambda e: e.reciprocal(out=u3[:, :], in_=u2[:, :]))
        VS(bm, lambda e: e.tensor_tensor(out=u4[:, :], in0=u1[:, :], in1=u3[:, :], op=ALU.mult))
        VS(bm, lambda e: e.tensor_scalar_mul(out=u5[:, :], in0=k11im, scalar1=-1.0))
        VS(bm, lambda e: e.tensor_tensor(out=u5[:, :], in0=u5[:, :], in1=u3[:, :], op=ALU.mult))
        VS(bm, lambda e: e.tensor_tensor(out=u1[:, :], in0=k01re, in1=k10re, op=ALU.mult))
        VS(bm, lambda e: e.tensor_tensor(out=u2[:, :], in0=k01im, in1=k10im, op=ALU.mult))
        VS(bm, lambda e: e.tensor_tensor(out=u1[:, :], in0=u1[:, :], in1=u2[:, :], op=ALU.subtract))
        VS(bm, lambda e: e.tensor_tensor(out=u2[:, :], in0=k01re, in1=k10im, op=ALU.mult))
        VS(bm, lambda e: e.tensor_tensor(out=u3[:, :], in0=k01im, in1=k10re, op=ALU.mult))
        VS(bm, lambda e: e.tensor_tensor(out=u2[:, :], in0=u2[:, :], in1=u3[:, :], op=ALU.add))
        VS(bm, lambda e: e.tensor_tensor(out=u3[:, :], in0=u1[:, :], in1=u4[:, :], op=ALU.mult))
        VS(bm, lambda e: e.tensor_tensor(out=u6[:, :], in0=u2[:, :], in1=u5[:, :], op=ALU.mult))
        VS(bm, lambda e: e.tensor_tensor(out=u3[:, :], in0=u3[:, :], in1=u6[:, :], op=ALU.subtract))
        VS(bm, lambda e: e.tensor_tensor(out=u6[:, :], in0=u1[:, :], in1=u5[:, :], op=ALU.mult))
        VS(bm, lambda e: e.tensor_tensor(out=u1[:, :], in0=u2[:, :], in1=u4[:, :], op=ALU.mult))
        VS(bm, lambda e: e.tensor_tensor(out=u6[:, :], in0=u6[:, :], in1=u1[:, :], op=ALU.add))
        VS(bm, lambda e: e.tensor_tensor(out=u1[:, :], in0=k00re, in1=u3[:, :], op=ALU.subtract))
        VS(bm, lambda e: e.tensor_tensor(out=u2[:, :], in0=k00im, in1=u6[:, :], op=ALU.subtract))
        VS(bm, lambda e: e.tensor_tensor(out=u3[:, :], in0=cre_sb[:, :], in1=u1[:, :], op=ALU.mult))
        VS(bm, lambda e: e.tensor_tensor(out=u4[:, :], in0=cim_sb[:, :], in1=u2[:, :], op=ALU.mult))
        VS(bm, lambda e: e.tensor_tensor(out=arre[:, :], in0=u3[:, :], in1=u4[:, :], op=ALU.subtract))
        VS(bm, lambda e: e.tensor_tensor(out=u3[:, :], in0=cre_sb[:, :], in1=u2[:, :], op=ALU.mult))
        VS(bm, lambda e: e.tensor_tensor(out=u4[:, :], in0=cim_sb[:, :], in1=u1[:, :], op=ALU.mult))
        ARSL_D = VS(bm, lambda e: e.tensor_tensor(out=arim[:, :], in0=u3[:, :], in1=u4[:, :], op=ALU.add))
        W(bm, "gpsimd", sv, ARSL_D)
        DMA(bm, "gpsimd", c_agi, bass.AP(agar_in, 0, [[4, 128], [1, 4]]), arre[:, :])
        n_agi = DMA(bm, "gpsimd", c_agi, bass.AP(agar_in, 512, [[4, 128], [1, 4]]), arim[:, :])
        W(bm, "gpsimd", c_agi, n_agi)
        CC(bm, k_ar, agar_in.ap(), agar_out.ap())

        tiles = []
        for g in range(4):
            for mat in ("g", "u"):
                for hc8 in range(8):
                    tiles.append((mat, g * 8 + hc8))
        for half in range(2):
            for lmc in range(LMCN):
                tiles.append(("d", lmc, half))
        NGU = 64

        QT_D = {}
        CAST_D = {}
        AM_D = {}
        DOUT_D = {}
        AGW_D = {}

        CBASE = NF4[0] + sum((NF4[k] - NF4[k - 1]) / 2.0 for k in range(1, 16, 2))
        gmask = [0]
        MASKS_D = {}
        MASKG_D = {}
        STT_D = {}
        SM_LAST = {}
        GM_LAST = {}

        def plan_deq_tile(i):
            t_ = tiles[i]
            par = i % 2
            Wd = 2048 if t_[0] == "d" else 1792
            if i >= 2:
                W(bm, "sync", sv, CAST_D[i - 2])
            if t_[0] == "g":
                src = gq[t_[1] * 128:(t_[1] + 1) * 128, :]
            elif t_[0] == "u":
                src = uq[t_[1] * 128:(t_[1] + 1) * 128, :]
            else:
                _, lmc, half = t_
                src = dq[lmc * 128:(lmc + 1) * 128, half * 2048:(half + 1) * 2048]
            QT_D[i] = DMA(bm, "sync", c_qt2[par], qt[par][:, 0:Wd], src)
            if i >= 1:
                W(bm, "gpsimd", sv, DOUT_D[i - 1])
            if t_[0] in "gu":
                amsrc = amg if t_[0] == "g" else amu
                hc = t_[1]
                DMA(bm, "gpsimd", c_am, am[0:64, 0:Wd],
                    bass.AP(amsrc, (hc * 2) * MSL, [[0, 64], [1, Wd]]))
                AM_D[i] = DMA(bm, "gpsimd", c_am, am[64:128, 0:Wd],
                              bass.AP(amsrc, (hc * 2 + 1) * MSL, [[0, 64], [1, Wd]]))
            else:
                _, lmc, half = t_
                DMA(bm, "gpsimd", c_am, am[0:64, 0:Wd],
                    bass.AP(amd, (lmc * 2) * HD + half * 2048, [[0, 64], [1, Wd]]))
                AM_D[i] = DMA(bm, "gpsimd", c_am, am[64:128, 0:Wd],
                              bass.AP(amd, (lmc * 2 + 1) * HD + half * 2048, [[0, 64], [1, Wd]]))
            W(bm, "vector", c_qt2[par], QT_D[i])
            if i >= 1:
                W(bm, "vector", ss, SM_LAST[i - 1])
                W(bm, "vector", sg, GM_LAST[i - 1])
            CAST_D[i] = V(bm, lambda e, p=par, w=Wd: e.tensor_copy(out=qb[:, 0:w], in_=qt[p][:, 0:w]))
            if i >= 2:
                W(bm, "vector", c_agw2[par], AGW_D[i - 2])
            W(bm, "scalar", sv, CAST_D[i])
            W(bm, "gpsimd", sv, CAST_D[i])
            for hi, (o, w) in enumerate(((0, 1024), (1024, Wd - 1024))):
                for k in range(1, 16):
                    gk = gmask[0]
                    gmask[0] += 1
                    slot = gk % 3
                    if k % 2 == 1:
                        if gk >= 3:
                            W(bm, "scalar", sv, STT_D[gk - 3])
                        MASKS_D[gk] = S(bm, lambda e, sl=slot, o=o, w=w, j=(k - 1) // 2: e.activation(
                            mk[sl][:, 0:w], qb[:, o:o + w], AF.Sign, bias=sbias[:, j:j + 1], scale=1.0))
                        SM_LAST[i] = MASKS_D[gk]
                        W(bm, "vector", ss, MASKS_D[gk])
                    else:
                        if gk >= 3:
                            W(bm, "gpsimd", sv, STT_D[gk - 3])
                        MASKG_D[gk] = G(lambda e, sl=slot, o=o, w=w, k=k: e.tensor_scalar(
                            out=mk[sl][:, 0:w], in0=qb[:, o:o + w], scalar1=float(k),
                            scalar2=None, op0=ALU.is_ge))
                        GM_LAST[i] = MASKG_D[gk]
                        W(bm, "vector", sg, MASKG_D[gk])
                    if k == 1:
                        STT_D[gk] = VS(bm, lambda e, sl=slot, w=w: e.tensor_scalar(
                            out=dacc[:, 0:w], in0=mk[sl][:, 0:w],
                            scalar1=(NF4[1] - NF4[0]) / 2.0, scalar2=CBASE,
                            op0=ALU.mult, op1=ALU.add))
                    else:
                        STT_D[gk] = VS(bm, lambda e, sl=slot, w=w, k=k: e.scalar_tensor_tensor(
                            out=dacc[:, 0:w], in0=mk[sl][:, 0:w], scalar=coefc[:, k:k + 1],
                            in1=dacc[:, 0:w], op0=ALU.mult, op1=ALU.add))
                if hi == 0:
                    W(bm, "vector", c_am, AM_D[i])
                DOUT_D[i] = VS(bm, lambda e, p=par, o=o, w=w: e.tensor_tensor(
                    out=dout[p][:, o:o + w], in0=dacc[:, 0:w], in1=am[:, o:o + w], op=ALU.mult))
            W(bm, "gpsimd", sv, DOUT_D[i])
            if t_[0] in "gu":
                hc = t_[1]
                g_, hc8 = divmod(hc, 8)
                ch = 2 * g_ + (0 if t_[0] == "g" else 1)
                base = ch * GU_CH + hc8 * TILE_E
                dst = bass.AP(aggu_in, base, [[128, 128], [8 * TILE_E, LMCN], [1, 128]])
                srcw = dout[par][:, 0:1792].rearrange("p (l j) -> p l j", j=128)
            else:
                _, lmc, half = t_
                base = half * D_CH + lmc * 16 * TILE_E
                dst = bass.AP(agd_in, base, [[128, 128], [TILE_E, 16], [1, 128]])
                srcw = dout[par][:, 0:2048].rearrange("p (l j) -> p l j", j=128)
            AGW_D[i] = DMA(bm, "gpsimd", c_agw2[par], dst, srcw)
            if i < NGU and i % 8 == 7:
                ch = i // 8
                W(bm, "gpsimd", c_agw2[i % 2], AGW_D[i])
                CC(bm, k_gu[ch],
                   bass.AP(aggu_in, ch * GU_CH, [[1, GU_CH]]),
                   bass.AP(aggu_out, ch * 8 * GU_CH, [[1, 8 * GU_CH]]))
            if i >= NGU and (i - NGU) % LMCN == LMCN - 1:
                half = (i - NGU) // LMCN
                W(bm, "gpsimd", c_agw2[i % 2], AGW_D[i])
                CC(bm, k_d[half],
                   bass.AP(agd_in, half * D_CH, [[1, D_CH]]),
                   bass.AP(agd_out, half * 8 * D_CH, [[1, 8 * D_CH]]))

        sg = Ctr(st, nc, "sg")

        def G(f):
            sg.n += 1
            bm["gpsimd"].append(lambda e, f=f, s=sg.h: f(e).then_inc(s, 1))
            return sg.n

        XB_D = {}
        SQC = {}
        S2CH = {}
        S2_D = {}
        MU_D = {}
        MOUT_D = {}
        MREP_D = {}
        NORM_D = {}

        def plan_ln_front(b_):
            par = b_ % 2
            if b_ >= 1:
                W(bm, "gpsimd", sp, S2_D[b_ - 1])
                W(bm, "gpsimd", sv, NORM_D[b_ - 1])
            XB_D[b_] = DMA(bm, "gpsimd", c_xb,
                           xb[:, :].rearrange("p (h t) -> p h t", t=LNT),
                           bass.AP(xT, b_ * LNT, [[TOKC, 128], [128 * TOKC, HCN], [1, LNT]]))
            if b_ >= 2:
                W(bm, "tensor", ss, MU_D[b_ - 2])
            W(bm, "tensor", c_xb, XB_D[b_])
            for hc in range(HCN):
                TN(bm, lambda e, h=hc, p=par: e.matmul(
                    ps_s1[p][0:1, :], ones[:, 0:1], xb[:, h * LNT:(h + 1) * LNT],
                    start=(h == 0), stop=(h == HCN - 1)))
            for ch in range(4):
                W(bm, "scalar", c_xb, XB_D[b_])
                if (b_, ch) != (0, 0):
                    pb, pch = (b_, ch - 1) if ch else (b_ - 1, 3)
                    W(bm, "scalar", sp, S2CH[(pb, pch)])
                SQC[(b_, ch)] = S(bm, lambda e, c=ch:
                                  e.activation(xb2[:, :], xb[:, c * 1024:(c + 1) * 1024], AF.Square))
                W(bm, "tensor", ss, SQC[(b_, ch)])
                for j in range(8):
                    is_first = (ch == 0 and j == 0)
                    is_chunk_last = (j == 7)
                    is_last = (ch == 3 and j == 7)
                    if is_chunk_last:
                        S2CH[(b_, ch)] = TI(bm, lambda e, j_=j, p=par, f=is_first, l=is_last: e.matmul(
                            ps_s2[p][0:1, :], ones[:, 0:1], xb2[:, j_ * LNT:(j_ + 1) * LNT],
                            start=f, stop=l))
                    else:
                        TN(bm, lambda e, j_=j, p=par, f=is_first: e.matmul(
                            ps_s2[p][0:1, :], ones[:, 0:1], xb2[:, j_ * LNT:(j_ + 1) * LNT],
                            start=f, stop=False))
            S2_D[b_] = S2CH[(b_, 3)]

        def plan_ln_back(b_):
            par = b_ % 2
            W(bm, "scalar", sp, S2_D[b_])
            if b_ >= 1:
                W(bm, "scalar", c_misc, MOUT_D[b_ - 1])
            SS(bm, lambda e, p=par: e.mul(mursrow[0:1, 0:LNT], ps_s1[p][0:1, :], 1.0 / HD))
            MU_D[b_] = SS(bm, lambda e, p=par: e.mul(vt1[0:1, :], ps_s2[p][0:1, :], 1.0 / HD))
            W(bm, "gpsimd", ss, MU_D[b_])
            G(lambda e: e.tensor_tensor(out=vt2[0:1, :], in0=mursrow[0:1, 0:LNT],
                                        in1=mursrow[0:1, 0:LNT], op=ALU.mult))
            W(bm, "gpsimd", sg, sg.n)
            G(lambda e: e.tensor_tensor(out=vt1[0:1, :], in0=vt1[0:1, :],
                                        in1=vt2[0:1, :], op=ALU.subtract))
            W(bm, "gpsimd", sg, sg.n)
            n_var = G(lambda e: e.tensor_scalar_add(out=vt1[0:1, :], in0=vt1[0:1, :],
                                                    scalar1=1e-5))
            W(bm, "scalar", sg, n_var)
            SS(bm, lambda e: e.activation(lnrow[0:1, :], vt1[0:1, :], AF.Ln))
            n_rs = SS(bm, lambda e: e.activation(mursrow[0:1, LNT:2 * LNT], lnrow[0:1, :], AF.Exp, scale=-0.5))
            W(bm, "gpsimd", ss, n_rs)
            MOUT_D[b_] = DMA(bm, "gpsimd", c_misc,
                             bass.AP(bounce, 1024 + b_ * 256, [[1, 1], [1, 256]]), mursrow[0:1, :])
            W(bm, "gpsimd", c_misc, MOUT_D[b_])
            if b_ >= 1:
                W(bm, "gpsimd", sv, NORM_D[b_ - 1])
            MREP_D[b_] = DMA(bm, "gpsimd", c_mr, mursrep[:, :],
                             bass.AP(bounce, 1024 + b_ * 256, [[0, 128], [1, 256]]))
            W(bm, "vector", c_mr, MREP_D[b_])
            xnAP = bass.AP(xn, b_ * LNT, [[HCN * TOKC, 128], [TOKC, HCN], [1, LNT]])
            xbAP = bass.AP(xb, 0, [[4096, 128], [LNT, HCN], [1, LNT]])
            muAP = bass.AP(mursrep, 0, [[256, 128], [0, HCN], [1, LNT]])
            rsAP = bass.AP(mursrep, LNT, [[256, 128], [0, HCN], [1, LNT]])
            VS(bm, lambda e, o=xnAP, i0=xbAP, i1=muAP: e.tensor_tensor(
                out=o, in0=i0, in1=i1, op=ALU.subtract))
            NORM_D[b_] = VS(bm, lambda e, o=xnAP, i1=rsAP: e.tensor_tensor(
                out=o, in0=o, in1=i1, op=ALU.mult))

        def emit_ar_rebuild():
            WS(bm, "sync", k_ar, 1)
            n = 0
            for r in range(NCORE):
                DMA(bm, "sync", c_ar, arbre[:, 4 * r:4 * r + 4],
                    bass.AP(agar_out, r * 1024, [[4, 128], [1, 4]]))
                n = DMA(bm, "sync", c_ar, arbim[:, 4 * r:4 * r + 4],
                        bass.AP(agar_out, r * 1024 + 512, [[4, 128], [1, 4]]))
            return n

        IDFT_MM = {}

        def emit_idft(n_ar):
            W(bm, "tensor", c_ar, n_ar)
            for lc in range(32):
                for s_ in range(2):
                    buf = 2 * s_ + (lc % 2)
                    if lc >= 2:
                        W(bm, "scalar", sp, IDFT_MM[(lc - 2, s_)])
                    srcm = cosm if s_ == 0 else nsinm
                    cd = DMA(bm, "scalar", c_cos4[buf], costl[buf][:, :],
                             srcm[lc * 128:(lc + 1) * 128, :])
                    W(bm, "tensor", c_cos4[buf], cd)
                    arb = arbre if s_ == 0 else arbim
                    first = (lc == 0 and s_ == 0)
                    last = (lc == 31 and s_ == 1)
                    IDFT_MM[(lc, s_)] = TI(bm, lambda e, lc=lc, b=buf, a=arb, f=first, l=last: e.matmul(
                        ps_k[0:1, :], a[:, lc:lc + 1], costl[b][:, :], start=f, stop=l))

        KROW_D = [0]
        N_AGK = [0]

        def emit_krow_copy():
            W(bm, "vector", sp, IDFT_MM[(31, 1)])
            KROW_D[0] = V(bm, lambda e: e.tensor_copy(out=krow[:, :], in_=ps_k[0:1, :]))

        def emit_krow_dma():
            W(bm, "sync", sv, KROW_D[0])
            N_AGK[0] = DMA(bm, "sync", c_agk, agk_in.ap(), krow[:, :])

        def emit_cck():
            W(bm, "gpsimd", c_agk, N_AGK[0])
            CC(bm, k_k, agk_in.ap(), agk_out.ap())

        BEV_D = {}

        def emit_bands():
            WS(bm, "gpsimd", k_k, 1)
            nkf = 0
            for r in range(NCORE):
                nkf = DMA(bm, "gpsimd", c_kpd, kpad[0:1, 128 + 512 * r:128 + 512 * (r + 1)],
                          agk_out[r:r + 1, 0:512])
            FWD_D = {}
            W(bm, "sync", c_kpd, nkf)
            for d in range(32):
                if d >= 2:
                    W(bm, "sync", sv, BEV_D[d - 2])
                FWD_D[d] = DMA(bm, "sync", c_fwd2[d % 2], fwd[d % 2][:, :],
                               bass.AP(kpad, 128 * d + 1, [[1, 128], [1, 128]]))
                W(bm, "tensor", c_fwd2[d % 2], FWD_D[d])
                if d >= 2:
                    W(bm, "tensor", sv, BEV_D[d - 2])
                jmm = TI(bm, lambda e, d=d: e.matmul(
                    pc[d % 2][:, 0:128], jrev_sb[:, :], fwd[d % 2][:, :], start=True, stop=True))
                W(bm, "vector", sp, jmm)
                BEV_D[d] = V(bm, lambda e, d=d: e.tensor_copy(
                    out=bands[:, d * 128:(d + 1) * 128], in_=pc[d % 2][:, 0:128]))

        CMM_D = {}
        CEV_D = {}
        CSP_D = {}
        conv_first = [True]

        def emit_conv_tb(tb):
            for ic in range(HCN):
                idx = tb * HCN + ic
                slot = idx % 2
                if conv_first[0]:
                    W(bm, "tensor", sv, BEV_D[31])
                    conv_first[0] = False
                if ic == 0:
                    W(bm, "tensor", sv, NORM_D[4 * tb + 3])
                if idx >= 2:
                    W(bm, "tensor", sv, CEV_D[idx - 2])
                for jc in range(ic + 1):
                    d = ic - jc
                    if jc == ic:
                        CMM_D[idx] = TI(bm, lambda e, d=d, jc=jc, tb=tb, s=slot: e.matmul(
                            pc[s][:, :], bands[:, d * 128:(d + 1) * 128],
                            xn[:, jc * TOKC + tb * 512:jc * TOKC + tb * 512 + 512],
                            start=(jc == 0), stop=True))
                    else:
                        TN(bm, lambda e, d=d, jc=jc, tb=tb, s=slot: e.matmul(
                            pc[s][:, :], bands[:, d * 128:(d + 1) * 128],
                            xn[:, jc * TOKC + tb * 512:jc * TOKC + tb * 512 + 512],
                            start=(jc == 0), stop=False))
                W(bm, "vector", sp, CMM_D[idx])
                if idx >= 2:
                    W(bm, "vector", c_csp2[slot], CSP_D[idx - 2])
                CEV_D[idx] = V(bm, lambda e, ic=ic, tb=tb, s=slot: e.scalar_tensor_tensor(
                    out=cvst[s][:, :], in0=xn[:, ic * TOKC + tb * 512:ic * TOKC + tb * 512 + 512],
                    scalar=dcol[:, 0:1], in1=pc[s][:, :], op0=ALU.mult, op1=ALU.add))
                W(bm, "gpsimd", sv, CEV_D[idx])
                CSP_D[idx] = DMA(bm, "gpsimd", c_csp2[slot],
                                 conv_s[ic * 128:(ic + 1) * 128, tb * 512:(tb + 1) * 512],
                                 cvst[slot][:, :])

        ti = 0
        n_ar_rebuild = [0]
        for b_ in range(LNBN):
            plan_ln_front(b_)
            for _ in range(4):
                if ti < NGU:
                    plan_deq_tile(ti)
                    ti += 1
                if ti == 7:
                    n_ar_rebuild[0] = emit_ar_rebuild()
                if ti == 12:
                    emit_idft(n_ar_rebuild[0])
                if ti == 14:
                    emit_krow_copy()
                if ti == 18:
                    emit_krow_dma()
                if ti == 20:
                    emit_cck()
            plan_ln_back(b_)
            if b_ == 6:
                emit_bands()
                emit_conv_tb(0)
            if b_ == 9:
                emit_conv_tb(1)
            if b_ == 12:
                emit_conv_tb(2)
        while ti < NGU:
            plan_deq_tile(ti)
            ti += 1
        emit_conv_tb(3)
        NCONV = TBN * HCN
        CEV_LAST = CEV_D[NCONV - 1]
        CSP_FULL = (c_csp2[0].n, c_csp2[1].n)

        ps1.close()
        ac1.close()

        ac2 = ExitStack()
        sb2 = lambda name, shape, dt: ac2.enter_context(nc.sbuf_tensor(name, shape, dt))
        wp = [[sb2("wp%d_%d" % (m, i), [128, 2048], BF16) for i in range(2)] for m in range(2)]
        sgst = [sb2("sgst%d" % i, [128, 512], BF16) for i in range(2)]
        hst = [sb2("hst%d" % i, [128, TOKC], BF16) for i in range(2)]
        ps2 = ExitStack()
        ps8 = [ps2.enter_context(nc.psum_tensor("ps8_%d" % i, [128, 512], F32)) for i in range(8)]

        PANH_D = {}
        GRPH_D = {}
        SILU_D = {}
        HM_D = {}
        HSP_D = {}

        def emit_mc(mc):
            r, lmc = divmod(mc, LMCN)
            for H in range(2):
                p2 = (2 * mc + H) % 2
                if mc == 0 and H == 0:
                    for ch in range(8):
                        WS(bm, "sync", k_gu[ch], 1)
                    W(bm, "sync", sv, CEV_LAST)
                    W(bm, "sync", c_csp2[0], CSP_FULL[0])
                    W(bm, "sync", c_csp2[1], CSP_FULL[1])
                if 2 * mc + H >= 2:
                    W(bm, "sync", sp, GRPH_D[(mc - 1, H)])
                for mat in range(2):
                    for gi in range(2):
                        g_ = 2 * H + gi
                        ch = 2 * g_ + mat
                        base = ch * 8 * GU_CH + r * GU_CH + lmc * 8 * TILE_E
                        PANH_D[(mc, H)] = DMA(
                            bm, "sync", c_pan2[p2],
                            wp[mat][p2][:, gi * 1024:(gi + 1) * 1024]
                            .rearrange("p (h j) -> p h j", j=128),
                            bass.AP(aggu_out, base, [[128, 128], [TILE_E, 8], [1, 128]]))
                W(bm, "tensor", c_pan2[p2], PANH_D[(mc, H)])
                if mc == 0 and H == 0:
                    W(bm, "tensor", sv, CEV_LAST)
                for hc8 in range(16):
                    hc = 16 * H + hc8
                    for mat in range(2):
                        for tb in range(TBN):
                            bank = mat * 4 + tb
                            if mc >= 1 and H == 0 and hc8 == 0:
                                if mat == 0:
                                    W(bm, "tensor", ss, SILU_D[(mc - 1, tb)])
                                else:
                                    W(bm, "tensor", sv, HM_D[(mc - 1, tb)])
                            if hc8 == 15 and mat == 1 and tb == TBN - 1:
                                GRPH_D[(mc, H)] = TI(bm, lambda e, hc8=hc8, hc=hc, m=mat, p=p2, tb=tb, bk=bank: e.matmul(
                                    ps8[bk][:, :], wp[m][p][:, hc8 * 128:(hc8 + 1) * 128],
                                    xn[:, hc * TOKC + tb * 512:hc * TOKC + tb * 512 + 512],
                                    start=(hc == 0), stop=(hc == HCN - 1)))
                            else:
                                TN(bm, lambda e, hc8=hc8, hc=hc, m=mat, p=p2, tb=tb, bk=bank: e.matmul(
                                    ps8[bk][:, :], wp[m][p][:, hc8 * 128:(hc8 + 1) * 128],
                                    xn[:, hc * TOKC + tb * 512:hc * TOKC + tb * 512 + 512],
                                    start=(hc == 0), stop=(hc == HCN - 1)))
            for tb in range(TBN):
                if tb == 0:
                    W(bm, "scalar", sp, GRPH_D[(mc, 1)])
                prev = (mc, tb - 2) if tb >= 2 else ((mc - 1, tb + 2) if mc >= 1 else None)
                if prev is not None:
                    W(bm, "scalar", sv, HM_D[prev])
                SILU_D[(mc, tb)] = S(bm, lambda e, tb=tb: e.activation(
                    sgst[tb % 2][:, :], ps8[tb][:, :], AF.Silu))
                W(bm, "vector", ss, SILU_D[(mc, tb)])
                if tb == 0 and mc >= 2:
                    W(bm, "vector", c_hsp2[mc % 2], HSP_D[mc - 2])
                HM_D[(mc, tb)] = V(bm, lambda e, tb=tb, p=mc % 2: e.tensor_tensor(
                    out=hst[p][:, tb * 512:(tb + 1) * 512],
                    in0=sgst[tb % 2][:, :], in1=ps8[4 + tb][:, :], op=ALU.mult))
            W(bm, "gpsimd", sv, HM_D[(mc, 3)])
            HSP_D[mc] = DMA(bm, "gpsimd", c_hsp2[mc % 2], h_s[mc * 128:(mc + 1) * 128, :], hst[mc % 2][:, :])

        for k in range(6):
            plan_deq_tile(ti)
            ti += 1
        for mc in range(GMCN):
            emit_mc(mc)
            if ti < len(tiles):
                plan_deq_tile(ti)
                ti += 1
        assert ti == len(tiles)
        HSP_FULL = (c_hsp2[0].n, c_hsp2[1].n)
        AGW_FULL = (c_agw2[0].n, c_agw2[1].n)
        HM_LAST = HM_D[(GMCN - 1, 3)]

        ps2.close()
        ac2.close()
        ac_dq.close()
        ac_xn.close()

        ac3 = ExitStack()
        sb3 = lambda name, shape, dt: ac3.enter_context(nc.sbuf_tensor(name, shape, dt))
        hp = sb3("hp", [128, GMCN * 512], BF16)
        wdp = [sb3("wdp%d" % i, [128, GMCN * 128], BF16) for i in range(2)]
        cvt = [sb3("cvt%d" % i, [128, 512], F32) for i in range(2)]
        xt_ = [sb3("xt%d" % i, [128, 512], F32) for i in range(2)]
        ost = [sb3("ost%d" % i, [128, 512], F32) for i in range(2)]
        ps3 = ExitStack()
        psy = [ps3.enter_context(nc.psum_tensor("psy%d" % i, [128, 512], F32)) for i in range(4)]

        bd = new_block()
        HP_D = {}
        WDP_D = {}
        LD_D = {}
        YMM_D = {}
        ASM_D = {}
        ST_D = {}
        W(bd, "sync", c_hsp2[0], HSP_FULL[0])
        W(bd, "sync", c_hsp2[1], HSP_FULL[1])
        W(bd, "sync", c_agw2[0], AGW_FULL[0])
        W(bd, "sync", c_agw2[1], AGW_FULL[1])
        WS(bd, "sync", k_d[0], 1)
        WS(bd, "sync", k_d[1], 1)
        W(bd, "scalar", sv, HM_LAST)
        for tb in range(TBN):
            if tb >= 1:
                W(bd, "sync", sp, YMM_D[(tb - 1, HCN - 1)])
            HP_D[tb] = DMA(bd, "sync", c_hp, hp[:, :].rearrange("p (m t) -> p m t", t=512),
                           bass.AP(h_s, tb * 512, [[TOKC, 128], [128 * TOKC, GMCN], [1, 512]]))
            for hdc in range(HCN):
                idx = tb * HCN + hdc
                par = idx % 2
                half, hs = divmod(hdc, 16)
                if idx >= 2:
                    W(bd, "sync", sp, YMM_D[((idx - 2) // HCN, (idx - 2) % HCN)])
                for r_ in range(NCORE):
                    WDP_D[idx] = DMA(
                        bd, "sync", c_wdp2[par],
                        wdp[par][:, (r_ * LMCN) * 128:((r_ + 1) * LMCN) * 128]
                        .rearrange("p (l j) -> p l j", j=128),
                        bass.AP(agd_out, half * 8 * D_CH + r_ * D_CH + hs * TILE_E,
                                [[128, 128], [16 * TILE_E, LMCN], [1, 128]]))
                if idx >= 2:
                    W(bd, "scalar", sv, ASM_D[idx - 2])
                DMA(bd, "scalar", c_ld2[par], cvt[par][:, :],
                    conv_s[hdc * 128:(hdc + 1) * 128, tb * 512:(tb + 1) * 512])
                LD_D[idx] = DMA(bd, "scalar", c_ld2[par], xt_[par][:, :],
                                bass.AP(xT, (hdc * 128) * TOKC + tb * 512, [[TOKC, 128], [1, 512]]))
                slot = idx % 4
                W(bd, "tensor", c_hp, HP_D[tb])
                W(bd, "tensor", c_wdp2[par], WDP_D[idx])
                if idx >= 4:
                    W(bd, "tensor", sv, ASM_D[idx - 4])
                for mc in range(GMCN):
                    if mc == GMCN - 1:
                        YMM_D[(tb, hdc)] = TI(bd, lambda e, mc=mc, s=slot, p=par: e.matmul(
                            psy[s][:, :], wdp[p][:, mc * 128:(mc + 1) * 128],
                            hp[:, mc * 512:(mc + 1) * 512], start=False, stop=True))
                    else:
                        TN(bd, lambda e, mc=mc, s=slot, p=par: e.matmul(
                            psy[s][:, :], wdp[p][:, mc * 128:(mc + 1) * 128],
                            hp[:, mc * 512:(mc + 1) * 512], start=(mc == 0), stop=False))
                W(bd, "vector", sp, YMM_D[(tb, hdc)])
                W(bd, "vector", c_ld2[par], LD_D[idx])
                if idx >= 2:
                    W(bd, "vector", so2[par], ST_D[idx - 2])
                VS(bd, lambda e, s=slot, p=par: e.tensor_tensor(
                    out=ost[p][:, :], in0=psy[s][:, :], in1=cvt[p][:, :], op=ALU.add))
                ASM_D[idx] = V(bd, lambda e, p=par: e.tensor_tensor(
                    out=ost[p][:, :], in0=ost[p][:, :], in1=xt_[p][:, :], op=ALU.add))
                W(bd, "gpsimd", sv, ASM_D[idx])
                so2[par].n += 16
                ST_D[idx] = so2[par].n
                bd["gpsimd"].append(lambda e, hdc=hdc, tb=tb, p=par, s=so2[par].h:
                                    e.dma_start(out=yT[hdc * 128:(hdc + 1) * 128, tb * 512:(tb + 1) * 512],
                                                in_=ost[p][:, :]).then_inc(s, 16))

        W(bd, "sync", c_hp, c_hp.n)
        W(bd, "sync", c_wdp2[0], c_wdp2[0].n)
        W(bd, "sync", c_wdp2[1], c_wdp2[1].n)
        W(bd, "scalar", c_ld2[0], c_ld2[0].n)
        W(bd, "scalar", c_ld2[1], c_ld2[1].n)
        W(bd, "gpsimd", so2[0], so2[0].n)
        W(bd, "gpsimd", so2[1], so2[1].n)

        ps3.close()
        ac3.close()

        for bops in blocks:
            with nc.Block() as blk:
                for eng in ("sync", "scalar", "vector", "tensor", "gpsimd"):
                    lst = bops[eng]
                    if lst:
                        getattr(blk, eng)(lambda e, lst=lst: [f(e) for f in lst])

    return nc


_CACHE = {}


def _host_consts():
    if "c" in _CACHE:
        return _CACHE["c"]
    om = np.exp(-2j * np.pi * (np.arange(L) / L)).astype(np.complex64)
    one = np.complex64(1.0)
    r = ((one - om) / (one + om)).astype(np.complex64)
    cl = (np.complex64(2.0) / (one + om)).astype(np.complex64)

    def l32(a):
        return np.ascontiguousarray(a.astype(np.float32).reshape(32, 128).T)

    rre_h, rim_h = l32(r.real), l32(r.imag)
    cre_h, cim_h = l32(cl.real), l32(cl.imag)
    jrev_h = np.eye(128, dtype=np.float32)[::-1].astype(BF)
    cosl = []
    nsinl = []
    for c in range(NCORE):
        n0 = 512 * c
        ang = 2.0 * np.pi * np.outer(np.arange(L, dtype=np.float64),
                                     (n0 + np.arange(512, dtype=np.float64))) / L
        cosl.append(np.ascontiguousarray((np.cos(ang) / L).astype(np.float32).astype(BF)))
        nsinl.append(np.ascontiguousarray((-np.sin(ang) / L).astype(np.float32).astype(BF)))
    _CACHE["c"] = (rre_h, rim_h, cre_h, cim_h, jrev_h, cosl, nsinl)
    return _CACHE["c"]


def make_in_maps(inputs):
    x = np.asarray(inputs["L0_attn_u"], dtype=np.float32).reshape(NCORE * TOKC, HD)
    gate_q = np.asarray(inputs["L0_ffn_gate_q"])
    up_q = np.asarray(inputs["L0_ffn_up_q"])
    down_q = np.asarray(inputs["L0_ffn_down_q"])
    g_am = np.asarray(inputs["L0_ffn_gate_absmax"], dtype=np.float32)
    u_am = np.asarray(inputs["L0_ffn_up_absmax"], dtype=np.float32)
    d_am = np.asarray(inputs["L0_ffn_down_absmax"], dtype=np.float32)
    lam = np.asarray(inputs["L0_attn_Lambda_ri"], dtype=np.float32)
    p_ri = np.asarray(inputs["L0_attn_p_ri"], dtype=np.float32)
    q_ri = np.asarray(inputs["L0_attn_q_ri"], dtype=np.float32)
    vc_ri = np.asarray(inputs["L0_attn_Vc_ri"], dtype=np.float32)
    ct_ri = np.asarray(inputs["L0_attn_Ct_ri"], dtype=np.float32).reshape(NST, 2)
    bp_in = np.asarray(inputs["L0_attn_Bp"], dtype=np.float32).reshape(NST)
    d_in = np.asarray(inputs["L0_attn_D"], dtype=np.float32).reshape(1)
    ls_in = np.asarray(inputs["L0_attn_log_step"], dtype=np.float32).reshape(1)

    rre_h, rim_h, cre_h, cim_h, jrev_h, cosl, nsinl = _host_consts()

    amg_full = np.ascontiguousarray(g_am.reshape(MT, 64).T)
    amu_full = np.ascontiguousarray(u_am.reshape(MT, 64).T)
    amd_full = np.ascontiguousarray(d_am.reshape(HD, 224).T)

    in_maps = []
    for c in range(NCORE):
        msl = slice(c * MSL, (c + 1) * MSL)
        tsl = slice(c * TOKC, (c + 1) * TOKC)
        csl = slice(4 * c, 4 * c + 4)
        in_maps.append({
            "xT": np.ascontiguousarray(x[tsl].T),
            "gq": np.ascontiguousarray(gate_q[msl].astype(np.uint16).T),
            "uq": np.ascontiguousarray(up_q[msl].astype(np.uint16).T),
            "dq": np.ascontiguousarray(down_q[:, msl].astype(np.uint16).T),
            "amg": np.ascontiguousarray(amg_full[:, msl]),
            "amu": np.ascontiguousarray(amu_full[:, msl]),
            "amd": np.ascontiguousarray(amd_full[c * 28:(c + 1) * 28, :]),
            "rre": np.ascontiguousarray(rre_h[:, csl]),
            "rim": np.ascontiguousarray(rim_h[:, csl]),
            "cre": np.ascontiguousarray(cre_h[:, csl]),
            "cim": np.ascontiguousarray(cim_h[:, csl]),
            "lamre": np.ascontiguousarray(lam[:, 0]), "lamim": np.ascontiguousarray(lam[:, 1]),
            "ctre": np.ascontiguousarray(ct_ri[:, 0]), "ctim": np.ascontiguousarray(ct_ri[:, 1]),
            "pre_": np.ascontiguousarray(p_ri[:, 0]), "pim": np.ascontiguousarray(p_ri[:, 1]),
            "qre": np.ascontiguousarray(q_ri[:, 0]), "qim": np.ascontiguousarray(q_ri[:, 1]),
            "bp": bp_in,
            "vcre": np.ascontiguousarray(vc_ri[:, :, 0]),
            "vcim": np.ascontiguousarray(vc_ri[:, :, 1]),
            "dsc": d_in, "lsc": ls_in,
            "cosm": cosl[c], "nsinm": nsinl[c],
            "jrev": jrev_h,
        })

    return in_maps


def kernel(**inputs):
    if "nc" not in _CACHE:
        _CACHE["nc"] = build_program()
    nc = _CACHE["nc"]
    in_maps = make_in_maps(inputs)
    res = run_bass_kernel_spmd(nc, in_maps, core_ids=list(range(NCORE)))
    out = np.empty((NCORE * TOKC, HD), dtype=np.float32)
    for c in range(NCORE):
        out[c * TOKC:(c + 1) * TOKC] = res.results[c]["yT"].T
    return out.reshape(4, HD, HD)


# version 09 (final)
# speedup vs baseline: 2.0571x; 2.0571x over previous
import os
os.environ.setdefault("NEURON_SCRATCHPAD_PAGE_SIZE", "512")

import numpy as np
import ml_dtypes
from contextlib import ExitStack

import concourse.bass as bass
import concourse.mybir as mybir
from concourse.bass_utils import run_bass_kernel_spmd

F32 = mybir.dt.float32
BF16 = mybir.dt.bfloat16
U16 = mybir.dt.uint16
AF = mybir.ActivationFunctionType
ALU = mybir.AluOpType
AX = mybir.AxisListType
BF = ml_dtypes.bfloat16

NCORE = 8
TOKC = 2048
HD = 4096
MT = 14336
MSL = MT // NCORE
L = 4096
NST = 64
HCN = 32
LMCN = 14
GMCN = 112
TBN = 4
LNBN = 16
LNT = 128

NF4 = [-1.0, -0.6961928009986877, -0.5250730514526367, -0.39491748809814453,
       -0.28444138169288635, -0.18477343022823334, -0.09105003625154495, 0.0,
       0.07958029955625534, 0.16093020141124725, 0.24611230194568634,
       0.33791524171829224, 0.44070982933044434, 0.5626170039176941,
       0.7229568362236023, 1.0]

TILE_E = 128 * 128
GU_CH = LMCN * 8 * TILE_E
D_CH = LMCN * 16 * TILE_E


class Ctr:
    def __init__(self, stack, nc, name):
        self.h = stack.enter_context(nc.semaphore(name))
        self.n = 0


def build_program():
    nc = bass.Bass(target_bir_lowering=False, detect_race_conditions=False)

    xT = nc.dram_tensor("xT", [HD, TOKC], F32, kind="ExternalInput")
    gq = nc.dram_tensor("gq", [HD, MSL], U16, kind="ExternalInput")
    uq = nc.dram_tensor("uq", [HD, MSL], U16, kind="ExternalInput")
    dq = nc.dram_tensor("dq", [MSL, HD], U16, kind="ExternalInput")
    amg = nc.dram_tensor("amg", [64, MSL], F32, kind="ExternalInput")
    amu = nc.dram_tensor("amu", [64, MSL], F32, kind="ExternalInput")
    amd = nc.dram_tensor("amd", [28, HD], F32, kind="ExternalInput")
    rre = nc.dram_tensor("rre", [128, 4], F32, kind="ExternalInput")
    rim = nc.dram_tensor("rim", [128, 4], F32, kind="ExternalInput")
    cre = nc.dram_tensor("cre", [128, 4], F32, kind="ExternalInput")
    cim = nc.dram_tensor("cim", [128, 4], F32, kind="ExternalInput")
    lamre = nc.dram_tensor("lamre", [NST], F32, kind="ExternalInput")
    lamim = nc.dram_tensor("lamim", [NST], F32, kind="ExternalInput")
    ctre = nc.dram_tensor("ctre", [NST], F32, kind="ExternalInput")
    ctim = nc.dram_tensor("ctim", [NST], F32, kind="ExternalInput")
    pre_ = nc.dram_tensor("pre_", [NST], F32, kind="ExternalInput")
    pim = nc.dram_tensor("pim", [NST], F32, kind="ExternalInput")
    qre = nc.dram_tensor("qre", [NST], F32, kind="ExternalInput")
    qim = nc.dram_tensor("qim", [NST], F32, kind="ExternalInput")
    bp = nc.dram_tensor("bp", [NST], F32, kind="ExternalInput")
    vcre = nc.dram_tensor("vcre", [NST, NST], F32, kind="ExternalInput")
    vcim = nc.dram_tensor("vcim", [NST, NST], F32, kind="ExternalInput")
    dsc = nc.dram_tensor("dsc", [1], F32, kind="ExternalInput")
    lsc = nc.dram_tensor("lsc", [1], F32, kind="ExternalInput")
    cosm = nc.dram_tensor("cosm", [L, 512], BF16, kind="ExternalInput")
    nsinm = nc.dram_tensor("nsinm", [L, 512], BF16, kind="ExternalInput")
    jrev = nc.dram_tensor("jrev", [128, 128], BF16, kind="ExternalInput")

    yT = nc.dram_tensor("yT", [HD, TOKC], F32, kind="ExternalOutput")

    agar_in = nc.dram_tensor("agar_in", [1, 1024], BF16, kind="Internal")
    agar_out = nc.dram_tensor("agar_out", [8, 1024], BF16, kind="Internal")
    agk_in = nc.dram_tensor("agk_in", [1, 512], F32, kind="Internal")
    agk_out = nc.dram_tensor("agk_out", [8, 512], F32, kind="Internal")
    aggu_in = nc.dram_tensor("aggu_in", [8 * GU_CH // 1024, 1024], BF16, kind="Internal")
    aggu_out = nc.dram_tensor("aggu_out", [64 * GU_CH // 1024, 1024], BF16, kind="Internal")
    agd_in = nc.dram_tensor("agd_in", [2 * D_CH // 1024, 1024], BF16, kind="Internal")
    agd_out = nc.dram_tensor("agd_out", [16 * D_CH // 1024, 1024], BF16, kind="Internal")
    h_s = nc.dram_tensor("h_s", [MT, TOKC], BF16, kind="Internal")
    conv_s = nc.dram_tensor("conv_s", [HD, TOKC], F32, kind="Internal")
    kpad = nc.dram_tensor("kpad", [1, 4352], BF16, kind="Internal")
    bounce = nc.dram_tensor("bounce", [1, 8192], F32, kind="Internal")

    with ExitStack() as st:
        sv = Ctr(st, nc, "sv")
        ss = Ctr(st, nc, "ss")
        sp = Ctr(st, nc, "sp")
        c_cst = Ctr(st, nc, "c_cst")
        c_qt2 = [Ctr(st, nc, "c_qt%d" % i) for i in range(2)]
        c_cos4 = [Ctr(st, nc, "c_cos%d" % i) for i in range(4)]
        c_xb = Ctr(st, nc, "c_xb")
        c_am = Ctr(st, nc, "c_am")
        c_agw2 = [Ctr(st, nc, "c_agw%d" % i) for i in range(2)]
        c_mr = Ctr(st, nc, "c_mr")
        c_vr = Ctr(st, nc, "c_vr")
        c_s2 = Ctr(st, nc, "c_s2")
        c_misc = Ctr(st, nc, "c_misc")
        c_ar = Ctr(st, nc, "c_ar")
        c_agi = Ctr(st, nc, "c_agi")
        c_agk = Ctr(st, nc, "c_agk")
        c_kpd = Ctr(st, nc, "c_kpd")
        c_fwd2 = [Ctr(st, nc, "c_fwd%d" % i) for i in range(2)]
        c_pan2 = [Ctr(st, nc, "c_pan%d" % i) for i in range(2)]
        c_hsp2 = [Ctr(st, nc, "c_hsp%d" % i) for i in range(2)]
        c_csp2 = [Ctr(st, nc, "c_csp%d" % i) for i in range(2)]
        c_hp = Ctr(st, nc, "c_hp")
        c_wdp2 = [Ctr(st, nc, "c_wdp%d" % i) for i in range(2)]
        c_ld2 = [Ctr(st, nc, "c_ld%d" % i) for i in range(2)]
        so2 = [Ctr(st, nc, "so2_%d" % i) for i in range(2)]
        k_ar = st.enter_context(nc.semaphore("k_ar"))
        k_k = st.enter_context(nc.semaphore("k_k"))
        k_gu = [st.enter_context(nc.semaphore("k_gu%d" % i)) for i in range(8)]
        k_d = [st.enter_context(nc.semaphore("k_d%d" % i)) for i in range(2)]

        blocks = []

        def new_block():
            b = {k: [] for k in ("sync", "scalar", "vector", "tensor", "gpsimd")}
            blocks.append(b)
            return b

        def W(b, eng, ctr, val):
            b[eng].append(lambda e, s=ctr.h, v=val: e.wait_ge(s, v))

        def WS(b, eng, sem, val=1):
            b[eng].append(lambda e, s=sem, v=val: e.wait_ge(s, v))

        def DMA(b, eng, ctr, out, in_, **kw):
            ctr.n += 16
            b[eng].append(lambda e, o=out, i=in_, s=ctr.h, k=dict(kw):
                          e.dma_start(out=o, in_=i, **k).then_inc(s, 16))
            return ctr.n

        def DMAT(b, eng, ctr, out, in_):
            ctr.n += 16
            b[eng].append(lambda e, o=out, i=in_, s=ctr.h:
                          e.dma_start_transpose(out=o, in_=i).then_inc(s, 16))
            return ctr.n

        def V(b, f):
            sv.n += 1
            b["vector"].append(lambda e, f=f, s=sv.h: f(e).then_inc(s, 1))
            return sv.n

        def VS(b, f):
            n = V(b, f)
            W(b, "vector", sv, n)
            return n

        def S(b, f):
            ss.n += 1
            b["scalar"].append(lambda e, f=f, s=ss.h: f(e).then_inc(s, 1))
            return ss.n

        def SS(b, f):
            n = S(b, f)
            W(b, "scalar", ss, n)
            return n

        def TN(b, f):
            b["tensor"].append(lambda e, f=f: f(e))

        def TI(b, f):
            sp.n += 1
            b["tensor"].append(lambda e, f=f, s=sp.h: f(e).then_inc(s, 1))
            return sp.n

        def CC(b, sem, in_ap, out_ap):
            b["gpsimd"].append(lambda e, s=sem, i=in_ap, o=out_ap:
                               e.collective_compute(
                                   "AllGather", ALU.bypass,
                                   replica_groups=[list(range(NCORE))],
                                   ins=[i.opt()], outs=[o.opt()]).then_inc(s))

        ones = st.enter_context(nc.sbuf_tensor("ones", [128, 1], BF16))
        jrev_sb = st.enter_context(nc.sbuf_tensor("jrev_sb", [128, 128], BF16))
        dcol = st.enter_context(nc.sbuf_tensor("dcol", [128, 1], F32))
        zrow = st.enter_context(nc.sbuf_tensor("zrow", [1, 128], BF16))

        ac_xn = ExitStack()
        xn = ac_xn.enter_context(nc.sbuf_tensor("xn", [128, HCN * TOKC], BF16))

        ac_dq = ExitStack()
        sbq = lambda name, shape, dt: ac_dq.enter_context(nc.sbuf_tensor(name, shape, dt))
        qt = [sbq("qt%d" % i, [128, 2048], U16) for i in range(2)]
        qb = sbq("qb", [128, 2048], BF16)
        am = sbq("am", [128, 2048], F32)
        dacc = sbq("dacc", [128, 2048], BF16)
        dtt = sbq("dtt", [128, 2048], BF16)
        dout = [sbq("dout%d" % i, [128, 2048], BF16) for i in range(2)]

        b0 = new_block()
        DMA(b0, "sync", c_cst, jrev_sb[:, :], jrev.ap())
        DMA(b0, "sync", c_cst, dcol[:, :], bass.AP(dsc, 0, [[0, 128], [1, 1]]))
        V(b0, lambda e: e.memset(ones[:, :], 1.0))
        n_setup_v = V(b0, lambda e: e.memset(zrow[:, :], 0.0))
        W(b0, "gpsimd", sv, n_setup_v)
        DMA(b0, "gpsimd", c_misc, kpad[0:1, 0:128], zrow[0:1, :])
        DMA(b0, "gpsimd", c_misc, kpad[0:1, 4224:4352], zrow[0:1, :])
        W(b0, "sync", c_cst, c_cst.n)
        W(b0, "gpsimd", c_misc, c_misc.n)

        ac1 = ExitStack()
        sb = lambda name, shape, dt: ac1.enter_context(nc.sbuf_tensor(name, shape, dt))
        rre_sb = sb("rre_sb", [128, 4], F32)
        rim_sb = sb("rim_sb", [128, 4], F32)
        cre_sb = sb("cre_sb", [128, 4], F32)
        cim_sb = sb("cim_sb", [128, 4], F32)
        lamre_b = sb("lamre_b", [128, NST], F32)
        lamim_b = sb("lamim_b", [128, NST], F32)
        vc_re = sb("vc_re", [NST, NST], F32)
        vc_im = sb("vc_im", [NST, NST], F32)
        bpb = sb("bpb", [NST, NST], F32)
        colc = sb("colc", [NST, 8], F32)
        vtmp = sb("vtmp", [NST, 4], F32)
        vcols = sb("vcols", [NST, 8], F32)
        vrep = sb("vrep", [128, 8 * NST], F32)
        s2col = sb("s2col", [128, 1], F32)
        gcr = sb("gcr", [128, 1], F32)
        gci = sb("gci", [128, 1], F32)
        dre = sb("dre", [128, NST], F32)
        dim_ = sb("dim_", [128, NST], F32)
        d2 = sb("d2", [128, NST], F32)
        dt2 = sb("dt2", [128, NST], F32)
        inv = sb("inv", [128, NST], F32)
        dri = sb("dri", [128, NST], F32)
        dii = sb("dii", [128, NST], F32)
        kt1 = sb("kt1", [128, NST], F32)
        kt2 = sb("kt2", [128, NST], F32)
        kt3 = sb("kt3", [128, NST], F32)
        kacc = sb("kacc", [128, 8 * 4], F32)
        u1 = sb("u1", [128, 4], F32)
        u2 = sb("u2", [128, 4], F32)
        u3 = sb("u3", [128, 4], F32)
        u4 = sb("u4", [128, 4], F32)
        u5 = sb("u5", [128, 4], F32)
        u6 = sb("u6", [128, 4], F32)
        arre = sb("arre", [128, 4], F32)
        arim = sb("arim", [128, 4], F32)
        arbre = sb("arbre", [128, 32], BF16)
        arbim = sb("arbim", [128, 32], BF16)
        ls_sb = sb("ls_sb", [1, 1], F32)
        e_sb = sb("e_sb", [1, 1], F32)
        t2_sb = sb("t2_sb", [1, 1], F32)
        costl = [sb("costl%d" % i, [128, 512], BF16) for i in range(4)]
        krow = sb("krow", [1, 512], F32)
        bands = sb("bands", [128, 32 * 128], BF16)
        fwd = [sb("fwd%d" % i, [128, 128], BF16) for i in range(2)]
        cvst = [sb("cvst%d" % i, [128, 512], F32) for i in range(2)]
        xb = sb("xb", [128, 4096], BF16)
        xb2 = sb("xb2", [128, 1024], BF16)
        mursrow = sb("mursrow", [1, 256], F32)
        vt1 = sb("vt1", [1, 128], F32)
        vt2 = sb("vt2", [1, 128], F32)
        lnrow = sb("lnrow", [1, 128], F32)
        mursrep = sb("mursrep", [128, 256], F32)

        ps1 = ExitStack()
        ps_s1 = [ps1.enter_context(nc.psum_tensor("ps_s1%d" % i, [1, 128], F32)) for i in range(2)]
        ps_s2 = [ps1.enter_context(nc.psum_tensor("ps_s2%d" % i, [1, 128], F32)) for i in range(2)]
        ps_k = ps1.enter_context(nc.psum_tensor("ps_k", [1, 512], F32))
        pc = [ps1.enter_context(nc.psum_tensor("pc%d" % i, [128, 512], F32)) for i in range(2)]

        bm = new_block()

        for dst, src in ((rre_sb, rre), (rim_sb, rim), (cre_sb, cre), (cim_sb, cim),
                         (vc_re, vcre), (vc_im, vcim)):
            DMA(bm, "sync", c_cst, dst[:, :], src.ap())
        DMA(bm, "sync", c_cst, lamre_b[:, :], bass.AP(lamre, 0, [[0, 128], [1, NST]]))
        DMA(bm, "sync", c_cst, lamim_b[:, :], bass.AP(lamim, 0, [[0, 128], [1, NST]]))
        DMA(bm, "sync", c_cst, bpb[:, :], bass.AP(bp, 0, [[0, NST], [1, NST]]))
        for i, t in enumerate((ctre, ctim, pre_, pim, qre, qim)):
            DMA(bm, "sync", c_cst, colc[:, i:i + 1], bass.AP(t, 0, [[1, NST], [1, 1]]))
        DMA(bm, "sync", c_cst, ls_sb[:, :], bass.AP(lsc, 0, [[1, 1], [1, 1]]))
        N_ACONST = c_cst.n

        W(bm, "scalar", c_cst, N_ACONST)
        SS(bm, lambda e: e.activation(e_sb[:, :], ls_sb[:, :], AF.Exp, scale=-1.0))
        n_t2 = SS(bm, lambda e: e.mul(t2_sb[:, :], e_sb[:, :], 2.0))
        W(bm, "gpsimd", ss, n_t2)
        n_bs2 = DMA(bm, "gpsimd", c_misc, bounce[0:1, 512:513], t2_sb[:, :])
        W(bm, "gpsimd", c_misc, n_bs2)
        n_s2c = DMA(bm, "gpsimd", c_s2, s2col[:, :], bass.AP(bounce, 512, [[0, 128], [1, 1]]))

        W(bm, "vector", c_cst, N_ACONST)
        VS(bm, lambda e: e.tensor_tensor(out=dre[0:NST, :], in0=vc_re[:, :], in1=bpb[:, :], op=ALU.mult))
        VS(bm, lambda e: e.reduce_sum(out=colc[:, 6:7], in_=dre[0:NST, :], axis=AX.X))
        VS(bm, lambda e: e.tensor_tensor(out=dre[0:NST, :], in0=vc_im[:, :], in1=bpb[:, :], op=ALU.mult))
        VS(bm, lambda e: e.reduce_sum(out=colc[:, 7:8], in_=dre[0:NST, :], axis=AX.X))
        vdefs = [(0, 1, 6, 7), (0, 1, 2, 3), (4, 5, 6, 7), (4, 5, 2, 3)]
        nv = None
        for vi, (ia, ib, ic, id_) in enumerate(vdefs):
            VS(bm, lambda e, ia=ia, ic=ic: e.tensor_tensor(out=vtmp[:, 0:1], in0=colc[:, ia:ia + 1], in1=colc[:, ic:ic + 1], op=ALU.mult))
            VS(bm, lambda e, ib=ib, id_=id_: e.tensor_tensor(out=vtmp[:, 1:2], in0=colc[:, ib:ib + 1], in1=colc[:, id_:id_ + 1], op=ALU.mult))
            VS(bm, lambda e, vi=vi: e.tensor_tensor(out=vcols[:, 2 * vi:2 * vi + 1], in0=vtmp[:, 0:1], in1=vtmp[:, 1:2], op=ALU.add))
            VS(bm, lambda e, ia=ia, id_=id_: e.tensor_tensor(out=vtmp[:, 2:3], in0=colc[:, ia:ia + 1], in1=colc[:, id_:id_ + 1], op=ALU.mult))
            VS(bm, lambda e, ib=ib, ic=ic: e.tensor_tensor(out=vtmp[:, 3:4], in0=colc[:, ib:ib + 1], in1=colc[:, ic:ic + 1], op=ALU.mult))
            nv = VS(bm, lambda e, vi=vi: e.tensor_tensor(out=vcols[:, 2 * vi + 1:2 * vi + 2], in0=vtmp[:, 2:3], in1=vtmp[:, 3:4], op=ALU.subtract))
        W(bm, "gpsimd", sv, nv)
        nbo = 0
        for k in range(8):
            nbo = DMA(bm, "gpsimd", c_misc, bass.AP(bounce, k * 64, [[1, NST]]), vcols[:, k:k + 1])
        W(bm, "gpsimd", c_misc, nbo)
        nvr = 0
        for k in range(8):
            nvr = DMA(bm, "gpsimd", c_vr, vrep[:, k * NST:(k + 1) * NST],
                      bass.AP(bounce, k * 64, [[0, 128], [1, NST]]))
        W(bm, "vector", c_vr, nvr)
        W(bm, "vector", c_s2, n_s2c)
        for c in range(4):
            VS(bm, lambda e, c=c: e.tensor_tensor(out=gcr[:, :], in0=rre_sb[:, c:c + 1], in1=s2col[:, :], op=ALU.mult))
            VS(bm, lambda e, c=c: e.tensor_tensor(out=gci[:, :], in0=rim_sb[:, c:c + 1], in1=s2col[:, :], op=ALU.mult))
            VS(bm, lambda e: e.tensor_scalar(out=dre[:, :], in0=lamre_b[:, :], scalar1=-1.0,
                                             scalar2=gcr[:, 0:1], op0=ALU.mult, op1=ALU.add))
            VS(bm, lambda e: e.tensor_scalar(out=dim_[:, :], in0=lamim_b[:, :], scalar1=-1.0,
                                             scalar2=gci[:, 0:1], op0=ALU.mult, op1=ALU.add))
            VS(bm, lambda e: e.tensor_tensor(out=d2[:, :], in0=dre[:, :], in1=dre[:, :], op=ALU.mult))
            VS(bm, lambda e: e.tensor_tensor(out=dt2[:, :], in0=dim_[:, :], in1=dim_[:, :], op=ALU.mult))
            VS(bm, lambda e: e.tensor_tensor(out=d2[:, :], in0=d2[:, :], in1=dt2[:, :], op=ALU.add))
            VS(bm, lambda e: e.reciprocal(out=inv[:, :], in_=d2[:, :]))
            VS(bm, lambda e: e.tensor_tensor(out=dri[:, :], in0=dre[:, :], in1=inv[:, :], op=ALU.mult))
            VS(bm, lambda e: e.tensor_tensor(out=dii[:, :], in0=dim_[:, :], in1=inv[:, :], op=ALU.mult))
            for k4 in range(4):
                ar_ = vrep[:, (2 * k4) * NST:(2 * k4 + 1) * NST]
                ai_ = vrep[:, (2 * k4 + 1) * NST:(2 * k4 + 2) * NST]
                VS(bm, lambda e, a=ar_: e.tensor_tensor(out=kt1[:, :], in0=a, in1=dri[:, :], op=ALU.mult))
                VS(bm, lambda e, a=ai_: e.tensor_tensor(out=kt2[:, :], in0=a, in1=dii[:, :], op=ALU.mult))
                VS(bm, lambda e: e.tensor_tensor(out=kt3[:, :], in0=kt1[:, :], in1=kt2[:, :], op=ALU.add))
                VS(bm, lambda e, k4=k4, c=c: e.reduce_sum(out=kacc[:, (2 * k4) * 4 + c:(2 * k4) * 4 + c + 1], in_=kt3[:, :], axis=AX.X))
                VS(bm, lambda e, a=ai_: e.tensor_tensor(out=kt1[:, :], in0=a, in1=dri[:, :], op=ALU.mult))
                VS(bm, lambda e, a=ar_: e.tensor_tensor(out=kt2[:, :], in0=a, in1=dii[:, :], op=ALU.mult))
                VS(bm, lambda e: e.tensor_tensor(out=kt3[:, :], in0=kt1[:, :], in1=kt2[:, :], op=ALU.subtract))
                VS(bm, lambda e, k4=k4, c=c: e.reduce_sum(out=kacc[:, (2 * k4 + 1) * 4 + c:(2 * k4 + 1) * 4 + c + 1], in_=kt3[:, :], axis=AX.X))
        k00re, k00im = kacc[:, 0:4], kacc[:, 4:8]
        k01re, k01im = kacc[:, 8:12], kacc[:, 12:16]
        k10re, k10im = kacc[:, 16:20], kacc[:, 20:24]
        k11re, k11im = kacc[:, 24:28], kacc[:, 28:32]
        VS(bm, lambda e: e.tensor_scalar_add(out=u1[:, :], in0=k11re, scalar1=1.0))
        VS(bm, lambda e: e.tensor_tensor(out=u2[:, :], in0=u1[:, :], in1=u1[:, :], op=ALU.mult))
        VS(bm, lambda e: e.tensor_tensor(out=u3[:, :], in0=k11im, in1=k11im, op=ALU.mult))
        VS(bm, lambda e: e.tensor_tensor(out=u2[:, :], in0=u2[:, :], in1=u3[:, :], op=ALU.add))
        VS(bm, lambda e: e.reciprocal(out=u3[:, :], in_=u2[:, :]))
        VS(bm, lambda e: e.tensor_tensor(out=u4[:, :], in0=u1[:, :], in1=u3[:, :], op=ALU.mult))
        VS(bm, lambda e: e.tensor_scalar_mul(out=u5[:, :], in0=k11im, scalar1=-1.0))
        VS(bm, lambda e: e.tensor_tensor(out=u5[:, :], in0=u5[:, :], in1=u3[:, :], op=ALU.mult))
        VS(bm, lambda e: e.tensor_tensor(out=u1[:, :], in0=k01re, in1=k10re, op=ALU.mult))
        VS(bm, lambda e: e.tensor_tensor(out=u2[:, :], in0=k01im, in1=k10im, op=ALU.mult))
        VS(bm, lambda e: e.tensor_tensor(out=u1[:, :], in0=u1[:, :], in1=u2[:, :], op=ALU.subtract))
        VS(bm, lambda e: e.tensor_tensor(out=u2[:, :], in0=k01re, in1=k10im, op=ALU.mult))
        VS(bm, lambda e: e.tensor_tensor(out=u3[:, :], in0=k01im, in1=k10re, op=ALU.mult))
        VS(bm, lambda e: e.tensor_tensor(out=u2[:, :], in0=u2[:, :], in1=u3[:, :], op=ALU.add))
        VS(bm, lambda e: e.tensor_tensor(out=u3[:, :], in0=u1[:, :], in1=u4[:, :], op=ALU.mult))
        VS(bm, lambda e: e.tensor_tensor(out=u6[:, :], in0=u2[:, :], in1=u5[:, :], op=ALU.mult))
        VS(bm, lambda e: e.tensor_tensor(out=u3[:, :], in0=u3[:, :], in1=u6[:, :], op=ALU.subtract))
        VS(bm, lambda e: e.tensor_tensor(out=u6[:, :], in0=u1[:, :], in1=u5[:, :], op=ALU.mult))
        VS(bm, lambda e: e.tensor_tensor(out=u1[:, :], in0=u2[:, :], in1=u4[:, :], op=ALU.mult))
        VS(bm, lambda e: e.tensor_tensor(out=u6[:, :], in0=u6[:, :], in1=u1[:, :], op=ALU.add))
        VS(bm, lambda e: e.tensor_tensor(out=u1[:, :], in0=k00re, in1=u3[:, :], op=ALU.subtract))
        VS(bm, lambda e: e.tensor_tensor(out=u2[:, :], in0=k00im, in1=u6[:, :], op=ALU.subtract))
        VS(bm, lambda e: e.tensor_tensor(out=u3[:, :], in0=cre_sb[:, :], in1=u1[:, :], op=ALU.mult))
        VS(bm, lambda e: e.tensor_tensor(out=u4[:, :], in0=cim_sb[:, :], in1=u2[:, :], op=ALU.mult))
        VS(bm, lambda e: e.tensor_tensor(out=arre[:, :], in0=u3[:, :], in1=u4[:, :], op=ALU.subtract))
        VS(bm, lambda e: e.tensor_tensor(out=u3[:, :], in0=cre_sb[:, :], in1=u2[:, :], op=ALU.mult))
        VS(bm, lambda e: e.tensor_tensor(out=u4[:, :], in0=cim_sb[:, :], in1=u1[:, :], op=ALU.mult))
        ARSL_D = VS(bm, lambda e: e.tensor_tensor(out=arim[:, :], in0=u3[:, :], in1=u4[:, :], op=ALU.add))
        W(bm, "gpsimd", sv, ARSL_D)
        DMA(bm, "gpsimd", c_agi, bass.AP(agar_in, 0, [[4, 128], [1, 4]]), arre[:, :])
        n_agi = DMA(bm, "gpsimd", c_agi, bass.AP(agar_in, 512, [[4, 128], [1, 4]]), arim[:, :])
        W(bm, "gpsimd", c_agi, n_agi)
        CC(bm, k_ar, agar_in.ap(), agar_out.ap())

        tiles = []
        for g in range(4):
            for mat in ("g", "u"):
                for hc8 in range(8):
                    tiles.append((mat, g * 8 + hc8))
        for half in range(2):
            for lmc in range(LMCN):
                tiles.append(("d", lmc, half))
        NGU = 64

        QT_D = {}
        CAST_D = {}
        AM_D = {}
        DOUT_D = {}
        AGW_D = {}

        def plan_deq_tile(i):
            t_ = tiles[i]
            par = i % 2
            Wd = 2048 if t_[0] == "d" else 1792
            if i >= 2:
                W(bm, "sync", sv, CAST_D[i - 2])
            if t_[0] == "g":
                src = gq[t_[1] * 128:(t_[1] + 1) * 128, :]
            elif t_[0] == "u":
                src = uq[t_[1] * 128:(t_[1] + 1) * 128, :]
            else:
                _, lmc, half = t_
                src = dq[lmc * 128:(lmc + 1) * 128, half * 2048:(half + 1) * 2048]
            QT_D[i] = DMA(bm, "sync", c_qt2[par], qt[par][:, 0:Wd], src)
            if i >= 1:
                W(bm, "gpsimd", sv, DOUT_D[i - 1])
            if t_[0] in "gu":
                amsrc = amg if t_[0] == "g" else amu
                hc = t_[1]
                DMA(bm, "gpsimd", c_am, am[0:64, 0:Wd],
                    bass.AP(amsrc, (hc * 2) * MSL, [[0, 64], [1, Wd]]))
                AM_D[i] = DMA(bm, "gpsimd", c_am, am[64:128, 0:Wd],
                              bass.AP(amsrc, (hc * 2 + 1) * MSL, [[0, 64], [1, Wd]]))
            else:
                _, lmc, half = t_
                DMA(bm, "gpsimd", c_am, am[0:64, 0:Wd],
                    bass.AP(amd, (lmc * 2) * HD + half * 2048, [[0, 64], [1, Wd]]))
                AM_D[i] = DMA(bm, "gpsimd", c_am, am[64:128, 0:Wd],
                              bass.AP(amd, (lmc * 2 + 1) * HD + half * 2048, [[0, 64], [1, Wd]]))
            W(bm, "vector", c_qt2[par], QT_D[i])
            CAST_D[i] = V(bm, lambda e, p=par, w=Wd: e.tensor_copy(out=qb[:, 0:w], in_=qt[p][:, 0:w]))
            if i >= 2:
                W(bm, "vector", c_agw2[par], AGW_D[i - 2])
            V(bm, lambda e, w=Wd: e.tensor_scalar(
                out=dacc[:, 0:w], in0=qb[:, 0:w], scalar1=0.0,
                scalar2=float(NF4[0]), op0=ALU.is_equal, op1=ALU.mult))
            for k in range(1, 16):
                V(bm, lambda e, w=Wd, k=k: e.tensor_scalar(
                    out=dtt[:, 0:w], in0=qb[:, 0:w], scalar1=float(k),
                    scalar2=float(NF4[k]), op0=ALU.is_equal, op1=ALU.mult))
                V(bm, lambda e, w=Wd: e.tensor_tensor(
                    out=dacc[:, 0:w], in0=dacc[:, 0:w], in1=dtt[:, 0:w], op=ALU.add))
            W(bm, "vector", c_am, AM_D[i])
            DOUT_D[i] = V(bm, lambda e, p=par, w=Wd: e.tensor_tensor(
                out=dout[p][:, 0:w], in0=dacc[:, 0:w], in1=am[:, 0:w], op=ALU.mult))
            W(bm, "vector", sv, DOUT_D[i])
            W(bm, "gpsimd", sv, DOUT_D[i])
            if t_[0] in "gu":
                hc = t_[1]
                g_, hc8 = divmod(hc, 8)
                ch = 2 * g_ + (0 if t_[0] == "g" else 1)
                base = ch * GU_CH + hc8 * TILE_E
                dst = bass.AP(aggu_in, base, [[128, 128], [8 * TILE_E, LMCN], [1, 128]])
                srcw = dout[par][:, 0:1792].rearrange("p (l j) -> p l j", j=128)
            else:
                _, lmc, half = t_
                base = half * D_CH + lmc * 16 * TILE_E
                dst = bass.AP(agd_in, base, [[128, 128], [TILE_E, 16], [1, 128]])
                srcw = dout[par][:, 0:2048].rearrange("p (l j) -> p l j", j=128)
            AGW_D[i] = DMA(bm, "gpsimd", c_agw2[par], dst, srcw)
            if i < NGU and i % 8 == 7:
                ch = i // 8
                W(bm, "gpsimd", c_agw2[i % 2], AGW_D[i])
                CC(bm, k_gu[ch],
                   bass.AP(aggu_in, ch * GU_CH, [[1, GU_CH]]),
                   bass.AP(aggu_out, ch * 8 * GU_CH, [[1, 8 * GU_CH]]))
            if i >= NGU and (i - NGU) % LMCN == LMCN - 1:
                half = (i - NGU) // LMCN
                W(bm, "gpsimd", c_agw2[i % 2], AGW_D[i])
                CC(bm, k_d[half],
                   bass.AP(agd_in, half * D_CH, [[1, D_CH]]),
                   bass.AP(agd_out, half * 8 * D_CH, [[1, 8 * D_CH]]))

        sg = Ctr(st, nc, "sg")

        def G(f):
            sg.n += 1
            bm["gpsimd"].append(lambda e, f=f, s=sg.h: f(e).then_inc(s, 1))
            return sg.n

        XB_D = {}
        SQC = {}
        S2CH = {}
        S2_D = {}
        MU_D = {}
        MOUT_D = {}
        MREP_D = {}
        NORM_D = {}

        def plan_ln_front(b_):
            par = b_ % 2
            if b_ >= 1:
                W(bm, "gpsimd", sp, S2_D[b_ - 1])
                W(bm, "gpsimd", sv, NORM_D[b_ - 1])
            XB_D[b_] = DMA(bm, "gpsimd", c_xb,
                           xb[:, :].rearrange("p (h t) -> p h t", t=LNT),
                           bass.AP(xT, b_ * LNT, [[TOKC, 128], [128 * TOKC, HCN], [1, LNT]]))
            if b_ >= 2:
                W(bm, "tensor", ss, MU_D[b_ - 2])
            W(bm, "tensor", c_xb, XB_D[b_])
            for hc in range(HCN):
                TN(bm, lambda e, h=hc, p=par: e.matmul(
                    ps_s1[p][0:1, :], ones[:, 0:1], xb[:, h * LNT:(h + 1) * LNT],
                    start=(h == 0), stop=(h == HCN - 1)))
            for ch in range(4):
                W(bm, "scalar", c_xb, XB_D[b_])
                if (b_, ch) != (0, 0):
                    pb, pch = (b_, ch - 1) if ch else (b_ - 1, 3)
                    W(bm, "scalar", sp, S2CH[(pb, pch)])
                SQC[(b_, ch)] = S(bm, lambda e, c=ch:
                                  e.activation(xb2[:, :], xb[:, c * 1024:(c + 1) * 1024], AF.Square))
                W(bm, "tensor", ss, SQC[(b_, ch)])
                for j in range(8):
                    is_first = (ch == 0 and j == 0)
                    is_chunk_last = (j == 7)
                    is_last = (ch == 3 and j == 7)
                    if is_chunk_last:
                        S2CH[(b_, ch)] = TI(bm, lambda e, j_=j, p=par, f=is_first, l=is_last: e.matmul(
                            ps_s2[p][0:1, :], ones[:, 0:1], xb2[:, j_ * LNT:(j_ + 1) * LNT],
                            start=f, stop=l))
                    else:
                        TN(bm, lambda e, j_=j, p=par, f=is_first: e.matmul(
                            ps_s2[p][0:1, :], ones[:, 0:1], xb2[:, j_ * LNT:(j_ + 1) * LNT],
                            start=f, stop=False))
            S2_D[b_] = S2CH[(b_, 3)]

        def plan_ln_back(b_):
            par = b_ % 2
            W(bm, "scalar", sp, S2_D[b_])
            if b_ >= 1:
                W(bm, "scalar", c_misc, MOUT_D[b_ - 1])
            SS(bm, lambda e, p=par: e.mul(mursrow[0:1, 0:LNT], ps_s1[p][0:1, :], 1.0 / HD))
            MU_D[b_] = SS(bm, lambda e, p=par: e.mul(vt1[0:1, :], ps_s2[p][0:1, :], 1.0 / HD))
            W(bm, "gpsimd", ss, MU_D[b_])
            G(lambda e: e.tensor_tensor(out=vt2[0:1, :], in0=mursrow[0:1, 0:LNT],
                                        in1=mursrow[0:1, 0:LNT], op=ALU.mult))
            W(bm, "gpsimd", sg, sg.n)
            G(lambda e: e.tensor_tensor(out=vt1[0:1, :], in0=vt1[0:1, :],
                                        in1=vt2[0:1, :], op=ALU.subtract))
            W(bm, "gpsimd", sg, sg.n)
            n_var = G(lambda e: e.tensor_scalar_add(out=vt1[0:1, :], in0=vt1[0:1, :],
                                                    scalar1=1e-5))
            W(bm, "scalar", sg, n_var)
            SS(bm, lambda e: e.activation(lnrow[0:1, :], vt1[0:1, :], AF.Ln))
            n_rs = SS(bm, lambda e: e.activation(mursrow[0:1, LNT:2 * LNT], lnrow[0:1, :], AF.Exp, scale=-0.5))
            W(bm, "gpsimd", ss, n_rs)
            MOUT_D[b_] = DMA(bm, "gpsimd", c_misc,
                             bass.AP(bounce, 1024 + b_ * 256, [[1, 1], [1, 256]]), mursrow[0:1, :])
            W(bm, "gpsimd", c_misc, MOUT_D[b_])
            if b_ >= 1:
                W(bm, "gpsimd", sv, NORM_D[b_ - 1])
            MREP_D[b_] = DMA(bm, "gpsimd", c_mr, mursrep[:, :],
                             bass.AP(bounce, 1024 + b_ * 256, [[0, 128], [1, 256]]))
            W(bm, "vector", c_mr, MREP_D[b_])
            xnAP = bass.AP(xn, b_ * LNT, [[HCN * TOKC, 128], [TOKC, HCN], [1, LNT]])
            xbAP = bass.AP(xb, 0, [[4096, 128], [LNT, HCN], [1, LNT]])
            muAP = bass.AP(mursrep, 0, [[256, 128], [0, HCN], [1, LNT]])
            rsAP = bass.AP(mursrep, LNT, [[256, 128], [0, HCN], [1, LNT]])
            VS(bm, lambda e, o=xnAP, i0=xbAP, i1=muAP: e.tensor_tensor(
                out=o, in0=i0, in1=i1, op=ALU.subtract))
            NORM_D[b_] = VS(bm, lambda e, o=xnAP, i1=rsAP: e.tensor_tensor(
                out=o, in0=o, in1=i1, op=ALU.mult))

        def emit_ar_rebuild():
            WS(bm, "sync", k_ar, 1)
            n = 0
            for r in range(NCORE):
                DMA(bm, "sync", c_ar, arbre[:, 4 * r:4 * r + 4],
                    bass.AP(agar_out, r * 1024, [[4, 128], [1, 4]]))
                n = DMA(bm, "sync", c_ar, arbim[:, 4 * r:4 * r + 4],
                        bass.AP(agar_out, r * 1024 + 512, [[4, 128], [1, 4]]))
            return n

        IDFT_MM = {}

        def emit_idft(n_ar):
            W(bm, "tensor", c_ar, n_ar)
            for lc in range(32):
                for s_ in range(2):
                    buf = 2 * s_ + (lc % 2)
                    if lc >= 2:
                        W(bm, "scalar", sp, IDFT_MM[(lc - 2, s_)])
                    srcm = cosm if s_ == 0 else nsinm
                    cd = DMA(bm, "scalar", c_cos4[buf], costl[buf][:, :],
                             srcm[lc * 128:(lc + 1) * 128, :])
                    W(bm, "tensor", c_cos4[buf], cd)
                    arb = arbre if s_ == 0 else arbim
                    first = (lc == 0 and s_ == 0)
                    last = (lc == 31 and s_ == 1)
                    IDFT_MM[(lc, s_)] = TI(bm, lambda e, lc=lc, b=buf, a=arb, f=first, l=last: e.matmul(
                        ps_k[0:1, :], a[:, lc:lc + 1], costl[b][:, :], start=f, stop=l))

        KROW_D = [0]
        N_AGK = [0]

        def emit_krow_copy():
            W(bm, "vector", sp, IDFT_MM[(31, 1)])
            KROW_D[0] = V(bm, lambda e: e.tensor_copy(out=krow[:, :], in_=ps_k[0:1, :]))

        def emit_krow_dma():
            W(bm, "sync", sv, KROW_D[0])
            N_AGK[0] = DMA(bm, "sync", c_agk, agk_in.ap(), krow[:, :])

        def emit_cck():
            W(bm, "gpsimd", c_agk, N_AGK[0])
            CC(bm, k_k, agk_in.ap(), agk_out.ap())

        BEV_D = {}

        def emit_bands():
            WS(bm, "gpsimd", k_k, 1)
            nkf = 0
            for r in range(NCORE):
                nkf = DMA(bm, "gpsimd", c_kpd, kpad[0:1, 128 + 512 * r:128 + 512 * (r + 1)],
                          agk_out[r:r + 1, 0:512])
            FWD_D = {}
            W(bm, "sync", c_kpd, nkf)
            for d in range(32):
                if d >= 2:
                    W(bm, "sync", sv, BEV_D[d - 2])
                FWD_D[d] = DMA(bm, "sync", c_fwd2[d % 2], fwd[d % 2][:, :],
                               bass.AP(kpad, 128 * d + 1, [[1, 128], [1, 128]]))
                W(bm, "tensor", c_fwd2[d % 2], FWD_D[d])
                if d >= 2:
                    W(bm, "tensor", sv, BEV_D[d - 2])
                jmm = TI(bm, lambda e, d=d: e.matmul(
                    pc[d % 2][:, 0:128], jrev_sb[:, :], fwd[d % 2][:, :], start=True, stop=True))
                W(bm, "vector", sp, jmm)
                BEV_D[d] = V(bm, lambda e, d=d: e.tensor_copy(
                    out=bands[:, d * 128:(d + 1) * 128], in_=pc[d % 2][:, 0:128]))

        CMM_D = {}
        CEV_D = {}
        CSP_D = {}
        conv_first = [True]

        def emit_conv_tb(tb):
            for ic in range(HCN):
                idx = tb * HCN + ic
                slot = idx % 2
                if conv_first[0]:
                    W(bm, "tensor", sv, BEV_D[31])
                    conv_first[0] = False
                if ic == 0:
                    W(bm, "tensor", sv, NORM_D[4 * tb + 3])
                if idx >= 2:
                    W(bm, "tensor", sv, CEV_D[idx - 2])
                for jc in range(ic + 1):
                    d = ic - jc
                    if jc == ic:
                        CMM_D[idx] = TI(bm, lambda e, d=d, jc=jc, tb=tb, s=slot: e.matmul(
                            pc[s][:, :], bands[:, d * 128:(d + 1) * 128],
                            xn[:, jc * TOKC + tb * 512:jc * TOKC + tb * 512 + 512],
                            start=(jc == 0), stop=True))
                    else:
                        TN(bm, lambda e, d=d, jc=jc, tb=tb, s=slot: e.matmul(
                            pc[s][:, :], bands[:, d * 128:(d + 1) * 128],
                            xn[:, jc * TOKC + tb * 512:jc * TOKC + tb * 512 + 512],
                            start=(jc == 0), stop=False))
                W(bm, "vector", sp, CMM_D[idx])
                if idx >= 2:
                    W(bm, "vector", c_csp2[slot], CSP_D[idx - 2])
                CEV_D[idx] = V(bm, lambda e, ic=ic, tb=tb, s=slot: e.scalar_tensor_tensor(
                    out=cvst[s][:, :], in0=xn[:, ic * TOKC + tb * 512:ic * TOKC + tb * 512 + 512],
                    scalar=dcol[:, 0:1], in1=pc[s][:, :], op0=ALU.mult, op1=ALU.add))
                W(bm, "gpsimd", sv, CEV_D[idx])
                CSP_D[idx] = DMA(bm, "gpsimd", c_csp2[slot],
                                 conv_s[ic * 128:(ic + 1) * 128, tb * 512:(tb + 1) * 512],
                                 cvst[slot][:, :])

        ti = 0
        n_ar_rebuild = [0]
        for b_ in range(LNBN):
            plan_ln_front(b_)
            for _ in range(4):
                if ti < NGU:
                    plan_deq_tile(ti)
                    ti += 1
                if ti == 7:
                    n_ar_rebuild[0] = emit_ar_rebuild()
                if ti == 12:
                    emit_idft(n_ar_rebuild[0])
                if ti == 14:
                    emit_krow_copy()
                if ti == 18:
                    emit_krow_dma()
                if ti == 20:
                    emit_cck()
            plan_ln_back(b_)
            if b_ == 6:
                emit_bands()
                emit_conv_tb(0)
            if b_ == 9:
                emit_conv_tb(1)
            if b_ == 12:
                emit_conv_tb(2)
        while ti < NGU:
            plan_deq_tile(ti)
            ti += 1
        emit_conv_tb(3)
        NCONV = TBN * HCN
        CEV_LAST = CEV_D[NCONV - 1]
        CSP_FULL = (c_csp2[0].n, c_csp2[1].n)

        ps1.close()
        ac1.close()

        ac2 = ExitStack()
        sb2 = lambda name, shape, dt: ac2.enter_context(nc.sbuf_tensor(name, shape, dt))
        wp = [[sb2("wp%d_%d" % (m, i), [128, 2048], BF16) for i in range(2)] for m in range(2)]
        sgst = [sb2("sgst%d" % i, [128, 512], BF16) for i in range(2)]
        hst = [sb2("hst%d" % i, [128, TOKC], BF16) for i in range(2)]
        ps2 = ExitStack()
        ps8 = [ps2.enter_context(nc.psum_tensor("ps8_%d" % i, [128, 512], F32)) for i in range(8)]

        PANH_D = {}
        GRPH_D = {}
        SILU_D = {}
        HM_D = {}
        HSP_D = {}

        def emit_mc(mc):
            r, lmc = divmod(mc, LMCN)
            for H in range(2):
                p2 = (2 * mc + H) % 2
                if mc == 0 and H == 0:
                    for ch in range(8):
                        WS(bm, "sync", k_gu[ch], 1)
                    W(bm, "sync", sv, CEV_LAST)
                    W(bm, "sync", c_csp2[0], CSP_FULL[0])
                    W(bm, "sync", c_csp2[1], CSP_FULL[1])
                if 2 * mc + H >= 2:
                    W(bm, "sync", sp, GRPH_D[(mc - 1, H)])
                for mat in range(2):
                    for gi in range(2):
                        g_ = 2 * H + gi
                        ch = 2 * g_ + mat
                        base = ch * 8 * GU_CH + r * GU_CH + lmc * 8 * TILE_E
                        PANH_D[(mc, H)] = DMA(
                            bm, "sync", c_pan2[p2],
                            wp[mat][p2][:, gi * 1024:(gi + 1) * 1024]
                            .rearrange("p (h j) -> p h j", j=128),
                            bass.AP(aggu_out, base, [[128, 128], [TILE_E, 8], [1, 128]]))
                W(bm, "tensor", c_pan2[p2], PANH_D[(mc, H)])
                if mc == 0 and H == 0:
                    W(bm, "tensor", sv, CEV_LAST)
                for hc8 in range(16):
                    hc = 16 * H + hc8
                    for mat in range(2):
                        for tb in range(TBN):
                            bank = mat * 4 + tb
                            if mc >= 1 and H == 0 and hc8 == 0:
                                if mat == 0:
                                    W(bm, "tensor", ss, SILU_D[(mc - 1, tb)])
                                else:
                                    W(bm, "tensor", sv, HM_D[(mc - 1, tb)])
                            if hc8 == 15 and mat == 1 and tb == TBN - 1:
                                GRPH_D[(mc, H)] = TI(bm, lambda e, hc8=hc8, hc=hc, m=mat, p=p2, tb=tb, bk=bank: e.matmul(
                                    ps8[bk][:, :], wp[m][p][:, hc8 * 128:(hc8 + 1) * 128],
                                    xn[:, hc * TOKC + tb * 512:hc * TOKC + tb * 512 + 512],
                                    start=(hc == 0), stop=(hc == HCN - 1)))
                            else:
                                TN(bm, lambda e, hc8=hc8, hc=hc, m=mat, p=p2, tb=tb, bk=bank: e.matmul(
                                    ps8[bk][:, :], wp[m][p][:, hc8 * 128:(hc8 + 1) * 128],
                                    xn[:, hc * TOKC + tb * 512:hc * TOKC + tb * 512 + 512],
                                    start=(hc == 0), stop=(hc == HCN - 1)))
            for tb in range(TBN):
                if tb == 0:
                    W(bm, "scalar", sp, GRPH_D[(mc, 1)])
                prev = (mc, tb - 2) if tb >= 2 else ((mc - 1, tb + 2) if mc >= 1 else None)
                if prev is not None:
                    W(bm, "scalar", sv, HM_D[prev])
                SILU_D[(mc, tb)] = S(bm, lambda e, tb=tb: e.activation(
                    sgst[tb % 2][:, :], ps8[tb][:, :], AF.Silu))
                W(bm, "vector", ss, SILU_D[(mc, tb)])
                if tb == 0 and mc >= 2:
                    W(bm, "vector", c_hsp2[mc % 2], HSP_D[mc - 2])
                HM_D[(mc, tb)] = V(bm, lambda e, tb=tb, p=mc % 2: e.tensor_tensor(
                    out=hst[p][:, tb * 512:(tb + 1) * 512],
                    in0=sgst[tb % 2][:, :], in1=ps8[4 + tb][:, :], op=ALU.mult))
            W(bm, "gpsimd", sv, HM_D[(mc, 3)])
            HSP_D[mc] = DMA(bm, "gpsimd", c_hsp2[mc % 2], h_s[mc * 128:(mc + 1) * 128, :], hst[mc % 2][:, :])

        for k in range(6):
            plan_deq_tile(ti)
            ti += 1
        for mc in range(GMCN):
            emit_mc(mc)
            if ti < len(tiles):
                plan_deq_tile(ti)
                ti += 1
        assert ti == len(tiles)
        HSP_FULL = (c_hsp2[0].n, c_hsp2[1].n)
        AGW_FULL = (c_agw2[0].n, c_agw2[1].n)
        HM_LAST = HM_D[(GMCN - 1, 3)]

        ps2.close()
        ac2.close()
        ac_dq.close()
        ac_xn.close()

        ac3 = ExitStack()
        sb3 = lambda name, shape, dt: ac3.enter_context(nc.sbuf_tensor(name, shape, dt))
        hp = sb3("hp", [128, GMCN * 512], BF16)
        wdp = [sb3("wdp%d" % i, [128, GMCN * 128], BF16) for i in range(2)]
        cvt = [sb3("cvt%d" % i, [128, 512], F32) for i in range(2)]
        xt_ = [sb3("xt%d" % i, [128, 512], F32) for i in range(2)]
        ost = [sb3("ost%d" % i, [128, 512], F32) for i in range(2)]
        ps3 = ExitStack()
        psy = [ps3.enter_context(nc.psum_tensor("psy%d" % i, [128, 512], F32)) for i in range(4)]

        bd = new_block()
        HP_D = {}
        WDP_D = {}
        LD_D = {}
        YMM_D = {}
        ASM_D = {}
        ST_D = {}
        W(bd, "sync", c_hsp2[0], HSP_FULL[0])
        W(bd, "sync", c_hsp2[1], HSP_FULL[1])
        W(bd, "sync", c_agw2[0], AGW_FULL[0])
        W(bd, "sync", c_agw2[1], AGW_FULL[1])
        WS(bd, "sync", k_d[0], 1)
        WS(bd, "sync", k_d[1], 1)
        W(bd, "scalar", sv, HM_LAST)
        for tb in range(TBN):
            if tb >= 1:
                W(bd, "sync", sp, YMM_D[(tb - 1, HCN - 1)])
            HP_D[tb] = DMA(bd, "sync", c_hp, hp[:, :].rearrange("p (m t) -> p m t", t=512),
                           bass.AP(h_s, tb * 512, [[TOKC, 128], [128 * TOKC, GMCN], [1, 512]]))
            for hdc in range(HCN):
                idx = tb * HCN + hdc
                par = idx % 2
                half, hs = divmod(hdc, 16)
                if idx >= 2:
                    W(bd, "sync", sp, YMM_D[((idx - 2) // HCN, (idx - 2) % HCN)])
                for r_ in range(NCORE):
                    WDP_D[idx] = DMA(
                        bd, "sync", c_wdp2[par],
                        wdp[par][:, (r_ * LMCN) * 128:((r_ + 1) * LMCN) * 128]
                        .rearrange("p (l j) -> p l j", j=128),
                        bass.AP(agd_out, half * 8 * D_CH + r_ * D_CH + hs * TILE_E,
                                [[128, 128], [16 * TILE_E, LMCN], [1, 128]]))
                if idx >= 2:
                    W(bd, "scalar", sv, ASM_D[idx - 2])
                DMA(bd, "scalar", c_ld2[par], cvt[par][:, :],
                    conv_s[hdc * 128:(hdc + 1) * 128, tb * 512:(tb + 1) * 512])
                LD_D[idx] = DMA(bd, "scalar", c_ld2[par], xt_[par][:, :],
                                bass.AP(xT, (hdc * 128) * TOKC + tb * 512, [[TOKC, 128], [1, 512]]))
                slot = idx % 4
                W(bd, "tensor", c_hp, HP_D[tb])
                W(bd, "tensor", c_wdp2[par], WDP_D[idx])
                if idx >= 4:
                    W(bd, "tensor", sv, ASM_D[idx - 4])
                for mc in range(GMCN):
                    if mc == GMCN - 1:
                        YMM_D[(tb, hdc)] = TI(bd, lambda e, mc=mc, s=slot, p=par: e.matmul(
                            psy[s][:, :], wdp[p][:, mc * 128:(mc + 1) * 128],
                            hp[:, mc * 512:(mc + 1) * 512], start=False, stop=True))
                    else:
                        TN(bd, lambda e, mc=mc, s=slot, p=par: e.matmul(
                            psy[s][:, :], wdp[p][:, mc * 128:(mc + 1) * 128],
                            hp[:, mc * 512:(mc + 1) * 512], start=(mc == 0), stop=False))
                W(bd, "vector", sp, YMM_D[(tb, hdc)])
                W(bd, "vector", c_ld2[par], LD_D[idx])
                if idx >= 2:
                    W(bd, "vector", so2[par], ST_D[idx - 2])
                VS(bd, lambda e, s=slot, p=par: e.tensor_tensor(
                    out=ost[p][:, :], in0=psy[s][:, :], in1=cvt[p][:, :], op=ALU.add))
                ASM_D[idx] = V(bd, lambda e, p=par: e.tensor_tensor(
                    out=ost[p][:, :], in0=ost[p][:, :], in1=xt_[p][:, :], op=ALU.add))
                W(bd, "gpsimd", sv, ASM_D[idx])
                so2[par].n += 16
                ST_D[idx] = so2[par].n
                bd["gpsimd"].append(lambda e, hdc=hdc, tb=tb, p=par, s=so2[par].h:
                                    e.dma_start(out=yT[hdc * 128:(hdc + 1) * 128, tb * 512:(tb + 1) * 512],
                                                in_=ost[p][:, :]).then_inc(s, 16))

        W(bd, "sync", c_hp, c_hp.n)
        W(bd, "sync", c_wdp2[0], c_wdp2[0].n)
        W(bd, "sync", c_wdp2[1], c_wdp2[1].n)
        W(bd, "scalar", c_ld2[0], c_ld2[0].n)
        W(bd, "scalar", c_ld2[1], c_ld2[1].n)
        W(bd, "gpsimd", so2[0], so2[0].n)
        W(bd, "gpsimd", so2[1], so2[1].n)

        ps3.close()
        ac3.close()

        for bops in blocks:
            with nc.Block() as blk:
                for eng in ("sync", "scalar", "vector", "tensor", "gpsimd"):
                    lst = bops[eng]
                    if lst:
                        getattr(blk, eng)(lambda e, lst=lst: [f(e) for f in lst])

    return nc


_CACHE = {}


def _host_consts():
    if "c" in _CACHE:
        return _CACHE["c"]
    om = np.exp(-2j * np.pi * (np.arange(L) / L)).astype(np.complex64)
    one = np.complex64(1.0)
    r = ((one - om) / (one + om)).astype(np.complex64)
    cl = (np.complex64(2.0) / (one + om)).astype(np.complex64)

    def l32(a):
        return np.ascontiguousarray(a.astype(np.float32).reshape(32, 128).T)

    rre_h, rim_h = l32(r.real), l32(r.imag)
    cre_h, cim_h = l32(cl.real), l32(cl.imag)
    jrev_h = np.eye(128, dtype=np.float32)[::-1].astype(BF)
    cosl = []
    nsinl = []
    for c in range(NCORE):
        n0 = 512 * c
        ang = 2.0 * np.pi * np.outer(np.arange(L, dtype=np.float64),
                                     (n0 + np.arange(512, dtype=np.float64))) / L
        cosl.append(np.ascontiguousarray((np.cos(ang) / L).astype(np.float32).astype(BF)))
        nsinl.append(np.ascontiguousarray((-np.sin(ang) / L).astype(np.float32).astype(BF)))
    _CACHE["c"] = (rre_h, rim_h, cre_h, cim_h, jrev_h, cosl, nsinl)
    return _CACHE["c"]


def make_in_maps(inputs):
    x = np.asarray(inputs["L0_attn_u"], dtype=np.float32).reshape(NCORE * TOKC, HD)
    gate_q = np.asarray(inputs["L0_ffn_gate_q"])
    up_q = np.asarray(inputs["L0_ffn_up_q"])
    down_q = np.asarray(inputs["L0_ffn_down_q"])
    g_am = np.asarray(inputs["L0_ffn_gate_absmax"], dtype=np.float32)
    u_am = np.asarray(inputs["L0_ffn_up_absmax"], dtype=np.float32)
    d_am = np.asarray(inputs["L0_ffn_down_absmax"], dtype=np.float32)
    lam = np.asarray(inputs["L0_attn_Lambda_ri"], dtype=np.float32)
    p_ri = np.asarray(inputs["L0_attn_p_ri"], dtype=np.float32)
    q_ri = np.asarray(inputs["L0_attn_q_ri"], dtype=np.float32)
    vc_ri = np.asarray(inputs["L0_attn_Vc_ri"], dtype=np.float32)
    ct_ri = np.asarray(inputs["L0_attn_Ct_ri"], dtype=np.float32).reshape(NST, 2)
    bp_in = np.asarray(inputs["L0_attn_Bp"], dtype=np.float32).reshape(NST)
    d_in = np.asarray(inputs["L0_attn_D"], dtype=np.float32).reshape(1)
    ls_in = np.asarray(inputs["L0_attn_log_step"], dtype=np.float32).reshape(1)

    rre_h, rim_h, cre_h, cim_h, jrev_h, cosl, nsinl = _host_consts()

    amg_full = np.ascontiguousarray(g_am.reshape(MT, 64).T)
    amu_full = np.ascontiguousarray(u_am.reshape(MT, 64).T)
    amd_full = np.ascontiguousarray(d_am.reshape(HD, 224).T)

    in_maps = []
    for c in range(NCORE):
        msl = slice(c * MSL, (c + 1) * MSL)
        tsl = slice(c * TOKC, (c + 1) * TOKC)
        csl = slice(4 * c, 4 * c + 4)
        in_maps.append({
            "xT": np.ascontiguousarray(x[tsl].T),
            "gq": np.ascontiguousarray(gate_q[msl].astype(np.uint16).T),
            "uq": np.ascontiguousarray(up_q[msl].astype(np.uint16).T),
            "dq": np.ascontiguousarray(down_q[:, msl].astype(np.uint16).T),
            "amg": np.ascontiguousarray(amg_full[:, msl]),
            "amu": np.ascontiguousarray(amu_full[:, msl]),
            "amd": np.ascontiguousarray(amd_full[c * 28:(c + 1) * 28, :]),
            "rre": np.ascontiguousarray(rre_h[:, csl]),
            "rim": np.ascontiguousarray(rim_h[:, csl]),
            "cre": np.ascontiguousarray(cre_h[:, csl]),
            "cim": np.ascontiguousarray(cim_h[:, csl]),
            "lamre": np.ascontiguousarray(lam[:, 0]), "lamim": np.ascontiguousarray(lam[:, 1]),
            "ctre": np.ascontiguousarray(ct_ri[:, 0]), "ctim": np.ascontiguousarray(ct_ri[:, 1]),
            "pre_": np.ascontiguousarray(p_ri[:, 0]), "pim": np.ascontiguousarray(p_ri[:, 1]),
            "qre": np.ascontiguousarray(q_ri[:, 0]), "qim": np.ascontiguousarray(q_ri[:, 1]),
            "bp": bp_in,
            "vcre": np.ascontiguousarray(vc_ri[:, :, 0]),
            "vcim": np.ascontiguousarray(vc_ri[:, :, 1]),
            "dsc": d_in, "lsc": ls_in,
            "cosm": cosl[c], "nsinm": nsinl[c],
            "jrev": jrev_h,
        })

    return in_maps


def kernel(**inputs):
    if "nc" not in _CACHE:
        _CACHE["nc"] = build_program()
    nc = _CACHE["nc"]
    in_maps = make_in_maps(inputs)
    res = run_bass_kernel_spmd(nc, in_maps, core_ids=list(range(NCORE)))
    out = np.empty((NCORE * TOKC, HD), dtype=np.float32)
    for c in range(NCORE):
        out[c * TOKC:(c + 1) * TOKC] = res.results[c]["yT"].T
    return out.reshape(4, HD, HD)
